# version 03
# baseline (speedup 1.0000x reference)
import numpy as np
import ml_dtypes
import concourse.bass as bass
import concourse.mybir as mybir
from concourse.bass_utils import run_bass_kernel_spmd

F32 = mybir.dt.float32
BF16 = mybir.dt.bfloat16
AF = mybir.ActivationFunctionType
ALU = mybir.AluOpType
AX = mybir.AxisListType

S = 2048
E = 1024
NCH = 8
NPAIR = 8
NKT = 16
NSPAN = 4
SUB = 512
NSUB = 4
FH = 4
NG = 32
LN8 = 2.0794415416798357
MSHIFT = 60.0

SB = 16512

O_XBUF = 0
O_LNBUF = 65536
O_AVT = 98304
O_RAILS = 131072
O_STAT = 139264
O_QA0 = 147456
O_QA1 = 151552
O_KA0 = 155648
O_KA1 = 159744
O_VAUG = 163840
O_SQT = 168448
O_EXP0 = 180224
O_EXP1 = 184320
O_AVN = 188416
O_WSL = 192512
O_CONST = 208896

ENGS = ("pe", "act", "dve", "gp", "sy")
NPOOL = 8


class EM:

    def __init__(self):
        self.ops = {e: [] for e in ENGS}
        self.cnt = {e: 0 for e in ENGS}
        self.sem_names = ["pe", "act", "dve"] + \
            ["%s%d" % (q, i) for q in ("gp", "sy") for i in range(NPOOL)]
        self.scnt = {sname: 0 for sname in self.sem_names}

    def op(self, eng, fn, waits=(), inc=False):
        w = {}
        for ev in waits:
            if ev is None:
                continue
            for k, v in ev.items():
                assert v <= self.scnt[k], "wait on future event"
                if w.get(k, -1) < v:
                    w[k] = v
        evt = None
        if inc:
            if eng in ("gp", "sy"):
                i = self.cnt[eng]
                self.cnt[eng] += 1
                sname = "%s%d" % (eng, i % NPOOL)
                if i >= NPOOL:
                    prev = 16 * (i // NPOOL)
                    if w.get(sname, -1) < prev:
                        w[sname] = prev
                self.scnt[sname] += 16
                evt = {sname: self.scnt[sname]}
                self.ops[eng].append((fn, w, sname))
                return evt
            else:
                self.cnt[eng] += 1
                self.scnt[eng] += 1
                evt = {eng: self.scnt[eng]}
                self.ops[eng].append((fn, w, eng))
                return evt
        self.ops[eng].append((fn, w, None))
        return evt


def ev_max(*evs):
    out = {}
    for ev in evs:
        if ev is None:
            continue
        for k, v in ev.items():
            if out.get(k, -1) < v:
                out[k] = v
    return out or None


def build_program(debug=False, trunc=None):
    nc = bass.Bass("TRN2", target_bir_lowering=False)

    xt = nc.dram_tensor("xt", [2, E, S], F32, kind="ExternalInput").ap()
    wq = nc.dram_tensor("wq", [E, E], F32, kind="ExternalInput").ap()
    wk = nc.dram_tensor("wk", [E, E], F32, kind="ExternalInput").ap()
    wv = nc.dram_tensor("wv", [E, E], F32, kind="ExternalInput").ap()
    woT = nc.dram_tensor("woT", [E, E], F32, kind="ExternalInput").ap()
    wcp = nc.dram_tensor("wcp", [8, 128, 128, 128], F32, kind="ExternalInput").ap()
    mkT = nc.dram_tensor("mkT", [128, 1024], F32, kind="ExternalInput").ap()
    upT = nc.dram_tensor("upT", [128, 4096], F32, kind="ExternalInput").ap()
    y = nc.dram_tensor("y", [2, E, S], F32, kind="ExternalOutput").ap()
    wcpb = nc.dram_tensor("wcpb", [8, 128, 128, 128], BF16, kind="Internal").ap()
    xsp = nc.dram_tensor("xsp", [2, E, S], F32, kind="Internal").ap()

    dbg = {}
    if debug:
        for name, shape, dt in [
            ("d_ln1", [128, 16384], BF16), ("d_qa0", [128, 2048], BF16),
            ("d_ka0", [128, 2048], BF16), ("d_qa1", [128, 2048], BF16),
            ("d_ka1", [128, 2048], BF16),
            ("d_vaug", [128, 2080], BF16), ("d_avt", [128, 2048], BF16),
            ("d_x1", [E, S], F32), ("d_exp", [128, 128], BF16),
            ("d_ln2", [128, 4096], BF16), ("d_ht", [128, 4096], BF16),
            ("d_avn", [128, 512], F32), ("d_rc", [128, 2048], F32),
            ("d_avpsum", [128, 512], F32), ("d_exp0", [128, 2048], BF16),
        ]:
            dbg[name] = nc.dram_tensor(name, shape, dt, kind="ExternalOutput").ap()

    ident_np = np.eye(128, dtype=np.float32)
    mask_np = np.zeros((128, 128), dtype=ml_dtypes.bfloat16)
    for k in range(128):
        mask_np[k, k:] = 1.0
    ones128_np = np.ones((128, 1), dtype=ml_dtypes.bfloat16)
    ind0_np = np.zeros((128, 65), dtype=ml_dtypes.bfloat16)
    ind0_np[0:64, 64] = 1.0
    ind1_np = np.zeros((128, 65), dtype=ml_dtypes.bfloat16)
    ind1_np[64:128, 64] = 1.0
    onesf_np = np.ones((1, 128), dtype=np.float32)
    h_identf = nc.inline_tensor(ident_np, name="c_identf")
    h_mask = nc.inline_tensor(np.asarray(mask_np), name="c_mask")
    h_ones = nc.inline_tensor(np.asarray(ones128_np), name="c_ones")
    h_ind0 = nc.inline_tensor(np.asarray(ind0_np), name="c_ind0")
    h_ind1 = nc.inline_tensor(np.asarray(ind1_np), name="c_ind1")
    h_onesf = nc.inline_tensor(onesf_np, name="c_onesf")

    def sb(name, shape, dt, off):
        return nc.alloc_sbuf_tensor_at(name, shape, dt, offset=SB + off)

    xbuf = sb("xbuf", [128, 8 * 2048], F32, O_XBUF)
    lnbuf = sb("lnbuf", [128, 8 * 2048], BF16, O_LNBUF)
    avt = sb("avt", [128, 8 * 2048], BF16, O_AVT)
    hT = sb("hT", [128, 65536], BF16, O_XBUF)
    rails = sb("rails", [128, 4096], BF16, O_RAILS)
    stat = sb("stat", [128, 2048], F32, O_STAT)
    qa = [sb("qa0", [128, 2048], BF16, O_QA0), sb("qa1", [128, 2048], BF16, O_QA1)]
    ka = [sb("ka0", [128, 2048], BF16, O_KA0), sb("ka1", [128, 2048], BF16, O_KA1)]
    vaug = sb("vaug", [128, 2080], BF16, O_VAUG)
    sqt = sb("sqt", [128, 2048], BF16, O_SQT)
    xball = sb("xball", [128, 4096], BF16, O_SQT)
    expS = [sb("expS0", [128, 2048], BF16, O_EXP0), sb("expS1", [128, 2048], BF16, O_EXP1)]
    avn = sb("avn", [128, 512], F32, O_AVN)
    lnsq = sb("lnsq", [128, 1024], BF16, O_AVN + 2048)
    wslq = sb("wslq", [128, 1024], BF16, O_WSL)
    wslk = sb("wslk", [128, 1024], BF16, O_WSL + 2048)
    wslv = sb("wslv", [128, 1024], BF16, O_WSL + 4096)
    wost = [sb("wost0", [128, 1024], BF16, O_WSL + 6144),
            sb("wost1", [128, 1024], BF16, O_WSL + 8192)]
    x1b = [sb("x1A", [128, 4096], F32, O_QA0), sb("x1B", [128, 4096], F32, O_VAUG)]
    wbuf = sb("wbuf", [128, 4096], BF16, O_EXP0)
    outb = [sb("outA", [128, 512], F32, O_AVN), sb("outB", [128, 512], F32, O_AVN + 2048)]
    ln2 = sb("ln2", [128, 4096], BF16, O_WSL)
    xb512 = sb("xb512", [128, 512], BF16, O_WSL + 8192)
    xq512 = sb("xq512", [128, 512], BF16, O_WSL + 9216)
    makbuf = sb("makbuf", [128, 1024], BF16, O_EXP0)
    upbuf = sb("upbuf", [128, 4096], BF16, O_SQT)
    masktri = sb("masktri", [128, 128], BF16, O_CONST)
    ones1 = sb("ones1", [128, 1], BF16, O_CONST + 256)
    ind0 = sb("ind0", [128, 65], BF16, O_CONST + 288)
    ind1 = sb("ind1", [128, 65], BF16, O_CONST + 448)
    onesf = sb("onesf", [1, 128], F32, O_CONST + 608)
    crow = sb("crow", [128, 32], F32, O_CONST + 1120)
    identf = sb("identf", [128, 128], F32, O_CONST + 1248)

    PA = nc.alloc_psum_tensor("PA", [128, 2048], F32)
    PBb = [nc.alloc_psum_tensor("PB%d" % i, [128, 512], F32) for i in range(4)]

    em = EM()

    bank_w = {}
    bank_r = {}

    def bw(b):
        return bank_w.get(b)

    def br(b):
        return bank_r.get(b)

    def set_bw(b, ev):
        bank_w[b] = ev_max(bank_w.get(b), ev)

    def set_br(b, ev):
        bank_r[b] = ev_max(bank_r.get(b), ev)

    class _Stop(Exception):
        pass

    def _ck(name):
        if trunc == name:
            raise _Stop()

    try:
        e_c = []
        for dst, src in [(masktri, h_mask), (ones1, h_ones), (ind0, h_ind0),
                         (ind1, h_ind1), (onesf, h_onesf), (identf, h_identf)]:
            e_c.append(em.op("gp", lambda g, d=dst, s=src: g.dma_start(d[:], s.ap()), inc=True))
        e_consts = ev_max(*e_c)

        e_cr = []
        for col, val in [(0, 1.0 / 1024.0), (1, 1e-5), (2, 0.125), (4, -0.5), (5, 0.5)]:
            e_cr.append(em.op("dve", lambda v, c=col, x=val: v.memset(crow[:, c:c + 1], x), inc=True))
        e_crow = ev_max(*e_cr)


        e_mk = em.op("gp", lambda g: g.dma_start(makbuf[:], mkT), inc=True)
        e_up = em.op("gp", lambda g: g.dma_start(upbuf[:], upT), inc=True)

        prev_evac = {}
        rails_ev = []
        for G in range(NG):
            slot = G % 4
            col = slot * 512
            e_mm = None
            for t in range(4):
                e_mm = em.op("pe", lambda p, t=t, col=col, G=G: p.matmul(
                    PA[32 * t:32 * t + 32, col:col + 128],
                    makbuf[32 * t:32 * t + 32, 32 * G:32 * G + 32],
                    upbuf[32 * t:32 * t + 32, 128 * G:128 * G + 128],
                    start=True, stop=True, tile_position=(32 * t, 32 * t)),
                    waits=[e_mk, e_up, prev_evac.get(slot)], inc=(t == 3))
            e_ev = em.op("act", lambda a, col=col, G=G: a.copy(
                rails[:, G * 128:(G + 1) * 128], PA[:, col:col + 128]),
                waits=[e_mm], inc=True)
            prev_evac[slot] = e_ev
            rails_ev.append(e_ev)
        e_rails = ev_max(*rails_ev)
        e_mcomb_pe = em.cnt["pe"]
        e_mcomb = {"pe": e_mcomb_pe}
        if debug:
            for tz in (qa[0], qa[1], ka[0], ka[1], stat, avn, vaug,
                       expS[0], expS[1]):
                e_crow = ev_max(e_crow, em.op(
                    "dve", lambda v, t=tz: v.memset(t[:], 0.0),
                    waits=[e_mcomb, e_crow], inc=True))
        for b in range(4):
            set_br("A%d" % b, e_rails)

        wcp_pending = list(range(8))
        e_wcp = [None] * 8

        def emit_wcp_cast():
            if wcp_pending:
                ec = wcp_pending.pop(0)
                e_wcp[ec] = em.op("gp", lambda g, ec=ec: g.dma_start(wcpb[ec], wcp[ec]), inc=True)

        e_hT_last_read = None
        e_spill = [None, None]
        e_lnbuf_guard = None
        e_avt_guard = None
        e_x1A_readers = None
        e_x1B_readers = None
        e_exp_guard = [None, None]
        e_avnregion_guard = None
        e_wsl_guard = [None, None, None]
        e_wost_guard = [None, None]
        e_ln2_guard = None
        e_stat_r = None
        e_store_last = None

        for s in range(2):
            e_xld = []
            for c in range(NCH):
                w = [e_hT_last_read, e_spill[s - 1] if s == 1 else None]
                e_xld.append(em.op("sy", lambda g, c=c, s=s: g.dma_start(
                    xbuf[:, c * 2048:(c + 1) * 2048], xt[s, 128 * c:128 * (c + 1), :]),
                    waits=w, inc=True))

            e_ms = []
            e_ms.append(em.op("dve", lambda v: v.memset(ka[0][64:65, :], 1.0),
                              waits=[e_x1A_readers, e_crow], inc=True))
            e_ms.append(em.op("dve", lambda v: v.memset(ka[1][64:65, :], 1.0),
                              waits=[e_x1A_readers], inc=True))
            vv = vaug.ap().rearrange("p (t c) -> p t c", c=130)
            e_ms.append(em.op("dve", lambda v, a=vv: v.memset(a[:, :, 64:65], 1.0),
                              waits=[e_x1B_readers], inc=True))
            e_ms.append(em.op("dve", lambda v, a=vv: v.memset(a[:, :, 129:130], 1.0),
                              waits=[e_x1B_readers], inc=True))
            e_kaug_init = ev_max(*e_ms)

            e_ln_done = {}
            e_xball_r = None
            for sp in range(NSPAN):
                e_sx = None
                e_sq = None
                sqev = {}
                for c in range(NCH):
                    e1 = em.op("dve", lambda v, c=c, sp=sp: v.tensor_copy(
                        xball[:, c * 512:(c + 1) * 512],
                        xbuf[:, c * 2048 + sp * 512: c * 2048 + sp * 512 + 512]),
                        waits=[e_xld[c], e_xball_r, e_mcomb, e_consts, e_crow,
                               e_x1B_readers], inc=True)
                    e2 = em.op("dve", lambda v, c=c: v.tensor_mul(
                        lnsq[:, (c % 2) * 512:(c % 2) * 512 + 512],
                        xball[:, c * 512:(c + 1) * 512],
                        xball[:, c * 512:(c + 1) * 512]),
                        waits=[e1, e_avnregion_guard, sqev.get(c - 2)], inc=True)
                    e_sx = em.op("pe", lambda p, c=c: p.matmul(
                        PBb[0][0:1, 0:512], ones1[:], xball[:, c * 512:(c + 1) * 512],
                        start=(c == 0), stop=(c == NCH - 1)),
                        waits=[e1, br("B0")], inc=True)
                    e_sq = em.op("pe", lambda p, c=c: p.matmul(
                        PBb[1][0:1, 0:512], ones1[:], lnsq[:, (c % 2) * 512:(c % 2) * 512 + 512],
                        start=(c == 0), stop=(c == NCH - 1)),
                        waits=[e2, br("B1")], inc=True)
                    sqev[c] = e_sq
                set_bw("B0", e_sx)
                set_bw("B1", e_sq)
                e_xball_r = ev_max(e_sx, e_sq)
                e_mean = em.op("act", lambda a: a.activation(
                    stat[0:1, 512:1024], PBb[0][0:1, 0:512], AF.Copy, scale=crow[0:1, 0:1]),
                    waits=[e_sx, e_stat_r], inc=True)
                e_m2 = em.op("dve", lambda v: v.tensor_mul(
                    stat[0:1, 1024:1536], stat[0:1, 512:1024], stat[0:1, 512:1024]),
                    waits=[e_mean], inc=True)
                e_var = em.op("dve", lambda v: v.scalar_tensor_tensor(
                    stat[0:1, 1536:2048], PBb[1][0:1, 0:512], 1.0 / 1024.0,
                    stat[0:1, 1024:1536], ALU.mult, ALU.subtract),
                    waits=[e_sq, e_m2], inc=True)
                e_lnv = em.op("act", lambda a: a.activation(
                    stat[0:1, 1024:1536], stat[0:1, 1536:2048], AF.Ln,
                    bias=crow[0:1, 1:2]), waits=[e_var], inc=True)
                e_rstd = em.op("act", lambda a: a.activation(
                    stat[0:1, 1536:2048], stat[0:1, 1024:1536], AF.Exp,
                    scale=crow[0:1, 4:5]), waits=[e_lnv], inc=True)
                set_br("B0", e_mean)
                set_br("B1", e_var)
                e_stat_r = ev_max(e_mean, e_var)
                e_mb = em.op("pe", lambda p: p.matmul(
                    PA[:, 0:512], onesf[:], stat[0:1, 512:1024], start=True, stop=True),
                    waits=[e_mean, br("A0")], inc=True)
                e_rb = em.op("pe", lambda p: p.matmul(
                    PA[:, 512:1024], onesf[:], stat[0:1, 1536:2048], start=True, stop=True),
                    waits=[e_rstd, br("A1")], inc=True)
                set_bw("A0", e_mb)
                set_bw("A1", e_rb)
                last = None
                for c in range(NCH):
                    esub = em.op("dve", lambda v, c=c, sp=sp: v.tensor_tensor(
                        stat[:, 0:512], xbuf[:, c * 2048 + sp * 512: c * 2048 + sp * 512 + 512],
                        PA[:, 0:512], ALU.subtract),
                        waits=[e_mb, e_lnbuf_guard, last], inc=True)
                    last = em.op("dve", lambda v, c=c, sp=sp: v.tensor_tensor(
                        lnbuf[:, c * 2048 + sp * 512: c * 2048 + sp * 512 + 512],
                        stat[:, 0:512], PA[:, 512:1024], ALU.mult),
                        waits=[e_rb, esub], inc=True)
                    e_ln_done[(sp, c)] = last
                set_br("A0", last)
                set_br("A1", last)
            e_ln_all = ev_max(*[e_ln_done[k] for k in e_ln_done])
            e_lnbuf_guard = None

            if debug and s == 0:
                em.op("sy", lambda g: g.dma_start(dbg["d_ln1"], lnbuf[:]),
                      waits=[e_ln_all], inc=True)
            _ck("ln1")

            wq_r = wq.rearrange("(c p) n -> p c n", p=128)
            wk_r = wk.rearrange("(c p) n -> p c n", p=128)
            wv_r = wv.rearrange("(c p) n -> p c n", p=128)
            woT_r = woT.rearrange("(c p) n -> p c n", p=128)

            def load_wsl(p_idx):
                res = []
                for i, (buf, src) in enumerate([(wslq, wq_r), (wslk, wk_r), (wslv, wv_r)]):
                    res.append(em.op("gp", lambda g, b=buf, sr=src, p=p_idx: g.dma_start(
                        b[:], sr[:, :, p * 128:(p + 1) * 128]),
                        waits=[e_wsl_guard[i], e_ln2_guard], inc=True))
                return res

            e_wsl = load_wsl(0)
            emit_wcp_cast()

            sc_toggle = [0]
            pend_tr = []
            e_avt_ev = None
            e_qa_rd = [e_x1A_readers, e_x1A_readers]
            e_ka_rd = [e_x1A_readers, e_x1A_readers]
            e_sqt_rd = e_x1B_readers
            e_avn_rd = e_avnregion_guard

            for p_idx in range(NPAIR):
                e_kaug_ev = [{}, {}]
                e_ksq = [None] * NSPAN
                for sp in range(NSPAN):
                    bk = "A%d" % (sp % 2)
                    col = (sp % 2) * 512
                    e_mm = None
                    for c in range(NCH):
                        e_mm = em.op("pe", lambda p, c=c, sp=sp, col=col: p.matmul(
                            PA[:, col:col + 512], wslk[:, c * 128:(c + 1) * 128],
                            lnbuf[:, c * 2048 + sp * 512: c * 2048 + sp * 512 + 512],
                            start=(c == 0), stop=(c == NCH - 1)),
                            waits=[e_wsl[1], e_ln_done[(sp, c)], br(bk), e_kaug_init],
                            inc=(c == NCH - 1))
                    set_bw(bk, e_mm)
                    e_wsl_guard[1] = ev_max(e_wsl_guard[1], e_mm)
                    ev0 = em.op("act", lambda a, sp=sp, col=col: a.activation(
                        ka[0][0:64, sp * 512:(sp + 1) * 512], PA[0:64, col:col + 512],
                        AF.Copy, scale=crow[0:64, 2:3]),
                        waits=[e_mm, e_ka_rd[0]], inc=True)
                    ev1 = em.op("act", lambda a, sp=sp, col=col: a.activation(
                        expS[1][64:128, sp * 512:(sp + 1) * 512], PA[64:128, col:col + 512],
                        AF.Copy, scale=crow[64:128, 2:3]),
                        waits=[e_mm, e_exp_guard[1]], inc=True)
                    e_kaug_ev[0][sp] = ev0
                    e_kaug_ev[1][sp] = ev1
                    esq = em.op("act", lambda a, sp=sp, col=col: a.activation(
                        sqt[:, sp * 512:(sp + 1) * 512], PA[:, col:col + 512], AF.Square),
                        waits=[e_mm, e_sqt_rd], inc=True)
                    e_ksq[sp] = esq
                    set_br(bk, ev_max(ev0, ev1, esq))
                e_sqt_rd = None
                if p_idx == 0:
                    _ck("kproj")
                e_shiftK = em.op("gp", lambda g: g.dma_start(
                    ka[1][0:64, :], expS[1][64:128, :]),
                    waits=[ev_max(*[e_kaug_ev[1][x] for x in range(NSPAN)]), e_ka_rd[1]],
                    inc=True)
                e_kmax = [None, None]
                for sp in range(NSPAN):
                    e_st0 = em.op("pe", lambda p, sp=sp: p.matmul(
                        PA[0:65, 1024:1536], ind0[:], sqt[:, sp * 512:(sp + 1) * 512],
                        start=True, stop=True), waits=[e_ksq[sp], br("A2")], inc=True)
                    set_bw("A2", e_st0)
                    e_st1 = em.op("pe", lambda p, sp=sp: p.matmul(
                        PA[0:65, 1536:2048], ind1[:], sqt[:, sp * 512:(sp + 1) * 512],
                        start=True, stop=True), waits=[e_ksq[sp], br("A3")], inc=True)
                    set_bw("A3", e_st1)
                    em0 = em.op("dve", lambda v, sp=sp: v.reduce_max(
                        stat[64:65, 1030 + sp:1031 + sp], PA[64:65, 1024:1536], axis=AX.X),
                        waits=[e_st0], inc=True)
                    em1 = em.op("dve", lambda v, sp=sp: v.reduce_max(
                        stat[64:65, 1050 + sp:1051 + sp], PA[64:65, 1536:2048], axis=AX.X),
                        waits=[e_st1], inc=True)
                    e_kmax[0] = ev_max(e_kmax[0], em0)
                    e_kmax[1] = ev_max(e_kmax[1], em1)
                    set_br("A2", em0)
                    set_br("A3", em1)
                e_mx0 = em.op("dve", lambda v: v.reduce_max(
                    stat[64:65, 1040:1041], stat[64:65, 1030:1034], axis=AX.X),
                    waits=[e_kmax[0]], inc=True)
                e_mx1 = em.op("dve", lambda v: v.reduce_max(
                    stat[64:65, 1055:1056], stat[64:65, 1050:1054], axis=AX.X),
                    waits=[e_kmax[1]], inc=True)
                e_l0 = em.op("act", lambda a: a.activation(
                    stat[64:65, 1042:1043], stat[64:65, 1040:1041], AF.Ln),
                    waits=[e_mx0], inc=True)
                e_l1 = em.op("act", lambda a: a.activation(
                    stat[64:65, 1057:1058], stat[64:65, 1055:1056], AF.Ln),
                    waits=[e_mx1], inc=True)
                e_lnc0 = em.op("dve", lambda v: v.tensor_scalar(
                    stat[64:65, 1044:1045], stat[64:65, 1042:1043], 0.5, -LN8,
                    ALU.mult, ALU.add), waits=[e_l0], inc=True)
                e_lnc1 = em.op("dve", lambda v: v.tensor_scalar(
                    stat[64:65, 1059:1060], stat[64:65, 1057:1058], 0.5, -LN8,
                    ALU.mult, ALU.add), waits=[e_l1], inc=True)

                e_qaug_ev = [{}, {}]
                e_mrow = [{}, {}]
                for sp in range(NSPAN):
                    bk = "A%d" % (sp % 2)
                    col = (sp % 2) * 512
                    e_mm = None
                    for c in range(NCH):
                        e_mm = em.op("pe", lambda p, c=c, sp=sp, col=col: p.matmul(
                            PA[:, col:col + 512], wslq[:, c * 128:(c + 1) * 128],
                            lnbuf[:, c * 2048 + sp * 512: c * 2048 + sp * 512 + 512],
                            start=(c == 0), stop=(c == NCH - 1)),
                            waits=[e_wsl[0], br(bk)], inc=(c == NCH - 1))
                    set_bw(bk, e_mm)
                    e_wsl_guard[0] = ev_max(e_wsl_guard[0], e_mm)
                    ev0 = em.op("act", lambda a, sp=sp, col=col: a.copy(
                        qa[0][0:64, sp * 512:(sp + 1) * 512], PA[0:64, col:col + 512]),
                        waits=[e_mm, e_qa_rd[0]], inc=True)
                    ev1 = em.op("act", lambda a, sp=sp, col=col: a.copy(
                        expS[0][64:128, sp * 512:(sp + 1) * 512], PA[64:128, col:col + 512]),
                        waits=[e_mm, e_exp_guard[0]], inc=True)
                    e_qaug_ev[0][sp] = ev0
                    e_qaug_ev[1][sp] = ev1
                    esq = em.op("act", lambda a, sp=sp, col=col: a.activation(
                        sqt[:, sp * 512:(sp + 1) * 512], PA[:, col:col + 512], AF.Square),
                        waits=[e_mm], inc=True)
                    set_br(bk, ev_max(ev0, ev1, esq))
                    e_st0 = em.op("pe", lambda p, sp=sp: p.matmul(
                        PA[0:65, 1024:1536], ind0[:], sqt[:, sp * 512:(sp + 1) * 512],
                        start=True, stop=True), waits=[esq, br("A2")], inc=True)
                    set_bw("A2", e_st0)
                    e_st1 = em.op("pe", lambda p, sp=sp: p.matmul(
                        PA[0:65, 1536:2048], ind1[:], sqt[:, sp * 512:(sp + 1) * 512],
                        start=True, stop=True), waits=[esq, br("A3")], inc=True)
                    set_bw("A3", e_st1)
                    a0 = em.op("act", lambda a: a.activation(
                        stat[64:65, 0:512], PA[64:65, 1024:1536], AF.Ln),
                        waits=[e_st0, e_mrow[1].get(sp - 1)], inc=True)
                    set_br("A2", a0)
                    b0 = em.op("act", lambda a: a.activation(
                        stat[64:65, 512:1024], stat[64:65, 0:512], AF.Exp,
                        scale=crow[64:65, 5:6], bias=stat[64:65, 1044:1045]),
                        waits=[a0, e_lnc0], inc=True)
                    m0 = em.op("dve", lambda v, sp=sp: v.tensor_scalar(
                        qa[0][64:65, sp * 512:(sp + 1) * 512], stat[64:65, 512:1024],
                        MSHIFT, -1.0, ALU.subtract, ALU.mult),
                        waits=[b0, e_qa_rd[0]], inc=True)
                    a1 = em.op("act", lambda a: a.activation(
                        stat[64:65, 0:512], PA[64:65, 1536:2048], AF.Ln),
                        waits=[e_st1, m0], inc=True)
                    set_br("A3", a1)
                    b1 = em.op("act", lambda a: a.activation(
                        stat[64:65, 512:1024], stat[64:65, 0:512], AF.Exp,
                        scale=crow[64:65, 5:6], bias=stat[64:65, 1059:1060]),
                        waits=[a1, e_lnc1, m0], inc=True)
                    m1 = em.op("dve", lambda v, sp=sp: v.tensor_scalar(
                        qa[1][64:65, sp * 512:(sp + 1) * 512], stat[64:65, 512:1024],
                        MSHIFT, -1.0, ALU.subtract, ALU.mult),
                        waits=[b1, e_qa_rd[1]], inc=True)
                    e_mrow[0][sp] = m0
                    e_mrow[1][sp] = m1
                if p_idx == 0:
                    _ck("qproj")
                e_shiftQ = em.op("gp", lambda g: g.dma_start(
                    qa[1][0:64, :], expS[0][64:128, :]),
                    waits=[ev_max(*[e_qaug_ev[1][x] for x in range(NSPAN)]), e_qa_rd[1]],
                    inc=True)
                e_qaug_ev[1] = {sp: e_shiftQ for sp in range(NSPAN)}
                e_kaug_ev[1] = {x: e_shiftK for x in range(NSPAN)}
                e_exp_guard = [ev_max(e_exp_guard[0], e_shiftQ),
                               ev_max(e_exp_guard[1], e_shiftK)]
                e_qa_rd = [None, None]
                e_ka_rd = [None, None]

                e_vev = [None] * NKT
                for tt in range(NKT):
                    bk = "A%d" % (tt % 2)
                    col = (tt % 2) * 512
                    e_mm = None
                    for c in range(NCH):
                        e_mm = em.op("pe", lambda p, c=c, tt=tt, col=col: p.matmul(
                            PA[:, col:col + 128],
                            lnbuf[:, c * 2048 + tt * 128: c * 2048 + tt * 128 + 128],
                            wslv[:, c * 128:(c + 1) * 128],
                            start=(c == 0), stop=(c == NCH - 1)),
                            waits=[e_wsl[2], br(bk)], inc=(c == NCH - 1))
                    set_bw(bk, e_mm)
                    e_wsl_guard[2] = ev_max(e_wsl_guard[2], e_mm)
                    d0 = em.op("dve", lambda v, tt=tt, col=col: v.tensor_copy(
                        vaug[:, tt * 130:tt * 130 + 64], PA[:, col:col + 64]),
                        waits=[e_mm, e_kaug_init], inc=True)
                    d1 = em.op("dve", lambda v, tt=tt, col=col: v.tensor_copy(
                        vaug[:, tt * 130 + 65:tt * 130 + 129], PA[:, col + 64:col + 128]),
                        waits=[d0], inc=True)
                    e_vev[tt] = d1
                    set_br(bk, d1)

                if p_idx == 0:
                    _ck("vproj")
                if p_idx + 1 < NPAIR:
                    e_wsl = load_wsl(p_idx + 1)
                emit_wcp_cast()

                if debug and s == 0 and p_idx == 0:
                    em.op("sy", lambda g: g.dma_start(dbg["d_qa0"], qa[0][:]),
                          waits=[ev_max(*[e_qaug_ev[0][x] for x in range(4)]),
                                 ev_max(*[e_mrow[0][x] for x in range(4)])], inc=True)
                    em.op("sy", lambda g: g.dma_start(dbg["d_qa1"], qa[1][:]),
                          waits=[ev_max(*[e_qaug_ev[1][x] for x in range(4)]),
                                 ev_max(*[e_mrow[1][x] for x in range(4)])], inc=True)
                    em.op("sy", lambda g: g.dma_start(dbg["d_ka0"], ka[0][:]),
                          waits=[ev_max(*[e_kaug_ev[0][x] for x in range(4)]), e_kaug_init], inc=True)
                    em.op("sy", lambda g: g.dma_start(dbg["d_ka1"], ka[1][:]),
                          waits=[ev_max(*[e_kaug_ev[1][x] for x in range(4)]), e_kaug_init], inc=True)
                    em.op("sy", lambda g: g.dma_start(dbg["d_vaug"], vaug[:]),
                          waits=[ev_max(*e_vev)], inc=True)

                def drain_tr(item):
                    nonlocal e_avn_rd, e_avt_ev, e_avt_guard
                    b = item["b"]
                    tb = b - 1 if b > 0 else 0
                    thh = item["hh"]
                    tp = item["p"]
                    prow = thh * 64
                    e_tr = None
                    for qt in range(4 * b, 4 * b + 4):
                        e_tr = em.op("pe", lambda p, qt=qt, thh=thh, tb=tb: p.transpose(
                            PBb[tb][0:64 + thh * 64, (qt % 4) * 128:(qt % 4) * 128 + 128],
                            avn[:, (qt % 4) * 128:(qt % 4) * 128 + 64 + thh * 64],
                            identf[:]),
                            waits=[item["nrm"][qt], br("B%d" % tb)], inc=(qt % 4 == 3))
                    set_bw("B%d" % tb, e_tr)
                    e_ae = em.op("act", lambda a, tb=tb, b=b, tp=tp, prow=prow: a.copy(
                        avt[prow:prow + 64,
                            tp * 2048 + b * 512: tp * 2048 + b * 512 + 512],
                        PBb[tb][prow:prow + 64, 0:512]),
                        waits=[e_tr, e_avt_guard], inc=True)
                    set_br("B%d" % tb, e_ae)
                    e_avn_rd = ev_max(e_avn_rd, e_tr)
                    e_avt_ev = ev_max(e_avt_ev, e_ae)

                for hh in range(2):
                    q_t = qa[hh]
                    k_t = ka[hh]
                    rows = (0, 65)
                    e_exp_sp = {}
                    e_mask_kt = {}
                    e_bankdone = {}
                    e_nrm = {}
                    e_avlast = {}
                    e_dbg_stat = None
                    for kt in range(NKT):
                        q0 = kt * 128
                        nsp = -(-(S - q0) // 512)
                        for j in range(nsp):
                            c0 = q0 + j * 512
                            n = min(512, S - c0)
                            bsl = 2 + sc_toggle[0]
                            sc_toggle[0] ^= 1
                            bk = "A%d" % bsl
                            col = bsl * 512
                            spans = list(range(c0 // 512, (c0 + n - 1) // 512 + 1))
                            wts = [br(bk), e_kaug_ev[hh].get(kt // 4),
                                   ev_max(*[e_qaug_ev[hh][x] for x in spans]),
                                   ev_max(*[e_mrow[hh][x] for x in spans])]
                            e_mm = em.op("pe", lambda p, kt=kt, c0=c0, n=n, col=col,
                                         q_t=q_t, k_t=k_t, r=rows: p.matmul(
                                PA[:, col:col + n],
                                k_t[r[0]:r[1], kt * 128:kt * 128 + 128],
                                q_t[r[0]:r[1], c0:c0 + n],
                                start=True, stop=True),
                                waits=wts, inc=True)
                            set_bw(bk, e_mm)
                            e_ex = em.op("act", lambda a, kt=kt, c0=c0, n=n, col=col: a.activation(
                                expS[kt % 2][:, c0 - kt * 128:c0 - kt * 128 + n],
                                PA[:, col:col + n], AF.Exp),
                                waits=[e_mm, e_avlast.get(kt - 2),
                                       e_exp_guard[kt % 2]], inc=True)
                            e_exp_sp[(kt, j)] = e_ex
                            set_br(bk, e_ex)
                            if j == 0:
                                e_mask_kt[kt] = em.op("dve", lambda v, kt=kt: v.tensor_tensor(
                                    expS[kt % 2][:, 0:128], expS[kt % 2][:, 0:128],
                                    masktri[:], ALU.mult), waits=[e_ex], inc=True)
                        while pend_tr and pend_tr[0]["hseq"] < 2 * p_idx + hh:
                            drain_tr(pend_tr.pop(0))
                        last_ev = None
                        for qt in range(kt, NKT):
                            j = ((qt - kt) * 128) // 512
                            wts = [e_exp_sp[(kt, j)], e_vev[kt]]
                            if qt == kt:
                                wts.append(e_mask_kt[kt])
                            first = (kt == 0 and qt % 4 == 0)
                            if first:
                                wts.append(br("B%d" % (qt // 4)))
                            e_av = em.op("pe", lambda p, kt=kt, qt=qt, hh=hh: p.matmul(
                                PBb[qt // 4][:, (qt % 4) * 128:(qt % 4) * 128 + 65],
                                expS[kt % 2][:, (qt - kt) * 128:(qt - kt) * 128 + 128],
                                vaug[:, kt * 130 + hh * 65: kt * 130 + hh * 65 + 65],
                                start=(kt == 0 and qt % 4 == 0),
                                stop=(kt == qt and qt % 4 == 3)),
                                waits=wts, inc=True)
                            last_ev = e_av
                            if first:
                                set_bw("B%d" % (qt // 4), e_av)
                            if qt == kt:
                                e_bankdone[qt // 4] = ev_max(e_bankdone.get(qt // 4), e_av)
                        if debug and s == 0 and p_idx == 0 and hh == 0 and kt == 0:
                            edump = em.op("sy", lambda g: g.dma_start(
                                dbg["d_exp0"], expS[0][:]),
                                waits=[e_mask_kt[0], last_ev], inc=True)
                            last_ev = ev_max(last_ev, edump)
                        e_avlast[kt] = last_ev
                        e_exp_guard[kt % 2] = last_ev
                        if kt % 4 == 3:
                            b = kt // 4
                            if debug and s == 0 and p_idx == 0 and hh == 0 and b == 0:
                                e_cpy = None
                                for qt_ in range(4):
                                    e_cpy = em.op("dve", lambda v, qt_=qt_: v.tensor_copy(
                                        outb[1][:, qt_ * 128:qt_ * 128 + 65],
                                        PBb[0][:, qt_ * 128:qt_ * 128 + 65]),
                                        waits=[e_bankdone[0], e_cpy], inc=True)
                                for qt_ in range(4):
                                    em.op("sy", lambda g, qt_=qt_: g.dma_start(
                                        dbg["d_avpsum"][:, qt_ * 128:qt_ * 128 + 65],
                                        outb[1][:, qt_ * 128:qt_ * 128 + 65]),
                                        waits=[e_cpy], inc=True)
                            for qt in range(4 * b, 4 * b + 4):
                                rc = em.op("dve", lambda v, qt=qt: v.reciprocal(
                                    stat[:, 1536 + (qt % 4):1537 + (qt % 4)],
                                    PBb[qt // 4][:, (qt % 4) * 128 + 64:(qt % 4) * 128 + 65]),
                                    waits=[e_bankdone[b], e_dbg_stat], inc=True)
                                nr = em.op("dve", lambda v, qt=qt, hh=hh: v.tensor_single_scalar(
                                    avn[:, (qt % 4) * 128 + hh * 64:(qt % 4) * 128 + hh * 64 + 64],
                                    PBb[qt // 4][:, (qt % 4) * 128:(qt % 4) * 128 + 64],
                                    stat[:, 1536 + (qt % 4):1537 + (qt % 4)], ALU.mult),
                                    waits=[rc, e_avn_rd], inc=True)
                                e_nrm[qt] = nr
                            pend_tr.append({"b": b, "hh": hh, "p": p_idx, "hseq": 2 * p_idx + hh, "nrm": {qt: e_nrm[qt] for qt in range(4 * b, 4 * b + 4)}})
                            set_br("B%d" % b, ev_max(*[e_nrm[x] for x in range(4 * b, 4 * b + 4)]))
                            if debug and s == 0 and p_idx == 0 and hh == 0 and b == 0:
                                edmp = em.op("sy", lambda g: g.dma_start(
                                    dbg["d_avn"], avn[:]),
                                    waits=[e_nrm[3]], inc=True)
                                e_avn_rd = ev_max(e_avn_rd, edmp)
                                e_dbg_stat = em.op("sy", lambda g: g.dma_start(
                                    dbg["d_rc"], stat[:]), waits=[e_nrm[3]], inc=True)
                        if p_idx == 0 and hh == 0 and trunc == ("kt%d" % kt):
                            _ck("kt%d" % kt)
                        while pend_tr and (pend_tr[0]["hseq"] == 2 * p_idx + hh
                                           and kt >= 4 * pend_tr[0]["b"] + 5):
                            drain_tr(pend_tr.pop(0))
                            if p_idx == 0 and hh == 0 and trunc == ("tr%d" % kt):
                                _ck("tr%d" % kt)
                        if p_idx == 0 and hh == 0 and trunc == ("tr%d" % kt):
                            _ck("tr%d" % kt)
                    e_qa_rd[hh] = ev_max(e_qa_rd[hh], *[e for e in e_avlast.values() if e])
                    if p_idx == 0 and hh == 0:
                        _ck("head0")
                    if p_idx == 0 and hh == 1:
                        _ck("head1")
                    if p_idx == 1 and hh == 1:
                        _ck("pair1")
                    e_ka_rd[hh] = e_qa_rd[hh]
                    if debug and s == 0 and p_idx == 0 and hh == 0:
                        _ed = em.op("sy", lambda g: g.dma_start(dbg["d_exp"], expS[1][:, 0:128]),
                                    waits=[e_exp_sp[(15, 0)], e_mask_kt[15]], inc=True)
                        e_exp_guard[1] = ev_max(e_exp_guard[1], _ed)
            _ck("pairs")
            e_avt_guard = None
            e_avnregion_guard = None

            while pend_tr:
                drain_tr(pend_tr.pop(0))

            e_wo = [None, None]
            e_xupd = []
            for ec in range(NCH):
                e_wo[ec % 2] = em.op("gp", lambda g, ec=ec: g.dma_start(
                    wost[ec % 2][:], woT_r[:, :, ec * 128:(ec + 1) * 128]),
                    waits=[e_wost_guard[ec % 2]], inc=True)
                for sp in range(NSPAN):
                    bk = "A%d" % (sp % 2)
                    col = (sp % 2) * 512
                    e_mm = None
                    for hc in range(NCH):
                        e_mm = em.op("pe", lambda p, ec=ec, sp=sp, hc=hc, col=col: p.matmul(
                            PA[:, col:col + 512], wost[ec % 2][:, hc * 128:(hc + 1) * 128],
                            avt[:, hc * 2048 + sp * 512: hc * 2048 + sp * 512 + 512],
                            start=(hc == 0), stop=(hc == NCH - 1)),
                            waits=[e_wo[ec % 2], e_avt_ev, br(bk)], inc=(hc == NCH - 1))
                    set_bw(bk, e_mm)
                    e_wost_guard[ec % 2] = ev_max(e_wost_guard[ec % 2], e_mm)
                    e_up2 = em.op("dve", lambda v, ec=ec, sp=sp, col=col: v.tensor_add(
                        xbuf[:, ec * 2048 + sp * 512: ec * 2048 + sp * 512 + 512],
                        xbuf[:, ec * 2048 + sp * 512: ec * 2048 + sp * 512 + 512],
                        PA[:, col:col + 512]), waits=[e_mm], inc=True)
                    set_br(bk, e_up2)
                    e_xupd.append(e_up2)
            e_xupd_all = ev_max(*e_xupd)

            e_sp_list = []
            for c in range(NCH):
                e_sp_list.append(em.op("sy", lambda g, c=c, s=s: g.dma_start(
                    xsp[s, 128 * c:128 * (c + 1), :], xbuf[:, c * 2048:(c + 1) * 2048]),
                    waits=[e_xupd_all], inc=True))
            e_spill[s] = ev_max(*e_sp_list)

            e_dbg_spill = None
            if debug and s == 0:
                _d1 = em.op("sy", lambda g: g.dma_start(dbg["d_x1"], xsp[0]),
                            waits=[e_spill[s]], inc=True)
                _d2 = em.op("sy", lambda g: g.dma_start(dbg["d_avt"], avt[:, 0:2048]),
                            waits=[e_avt_ev], inc=True)
                e_dbg_spill = ev_max(_d1, _d2)
            _ck("oproj")

            e_cp_last = None
            e_x1ld = [None, None]
            e_x1_rd = [None, None]
            e_ln2_rd = None
            wbuf_rd = [None, None]
            e_outb_rd = [None, None]
            xsp_r = xsp[s].rearrange("(c p) t -> p c t", p=128)
            x1v = [x1b[0].ap().rearrange("p (c t) -> p c t", t=512),
                   x1b[1].ap().rearrange("p (c t) -> p c t", t=512)]
            e_gelu_all = None

            for st in range(NSUB):
                xb = st % 2
                e_x1ld[xb] = em.op("sy", lambda g, st=st, xb=xb, xsp_r=xsp_r: g.dma_start(
                    x1v[xb], xsp_r[:, :, st * 512:(st + 1) * 512]),
                    waits=[e_spill[s], e_x1_rd[xb]], inc=True)
                e_sx = None
                e_sq = None
                e_xb5 = None
                for c in range(NCH):
                    w5 = [e_x1ld[xb], e_sx, e_sq]
                    if st == 0 and c == 0:
                        w5 += [e_wost_guard[1], e_ln2_guard]
                    e_xb5 = em.op("dve", lambda v, c=c, xb=xb: v.tensor_copy(
                        xb512[:, 0:512], x1b[xb][:, c * 512:(c + 1) * 512]),
                        waits=w5, inc=True)
                    e_xq5 = em.op("dve", lambda v: v.tensor_mul(
                        xq512[:, 0:512], xb512[:, 0:512], xb512[:, 0:512]),
                        waits=[e_xb5], inc=True)
                    e_sx = em.op("pe", lambda p, c=c: p.matmul(
                        PBb[0][0:1, 0:512], ones1[:], xb512[:, 0:512],
                        start=(c == 0), stop=(c == NCH - 1)),
                        waits=[e_xb5, br("B0")], inc=True)
                    e_sq = em.op("pe", lambda p, c=c: p.matmul(
                        PBb[1][0:1, 0:512], ones1[:], xq512[:, 0:512],
                        start=(c == 0), stop=(c == NCH - 1)),
                        waits=[e_xq5, br("B1")], inc=True)
                set_bw("B0", e_sx)
                set_bw("B1", e_sq)
                e_mean = em.op("act", lambda a: a.activation(
                    stat[0:1, 512:1024], PBb[0][0:1, 0:512], AF.Copy, scale=crow[0:1, 0:1]),
                    waits=[e_sx, e_stat_r], inc=True)
                e_m2 = em.op("dve", lambda v: v.tensor_mul(
                    stat[0:1, 1024:1536], stat[0:1, 512:1024], stat[0:1, 512:1024]),
                    waits=[e_mean], inc=True)
                e_var = em.op("dve", lambda v: v.scalar_tensor_tensor(
                    stat[0:1, 1536:2048], PBb[1][0:1, 0:512], 1.0 / 1024.0,
                    stat[0:1, 1024:1536], ALU.mult, ALU.subtract),
                    waits=[e_sq, e_m2], inc=True)
                e_lnv = em.op("act", lambda a: a.activation(
                    stat[0:1, 1024:1536], stat[0:1, 1536:2048], AF.Ln,
                    bias=crow[0:1, 1:2]), waits=[e_var], inc=True)
                e_rstd = em.op("act", lambda a: a.activation(
                    stat[0:1, 1536:2048], stat[0:1, 1024:1536], AF.Exp,
                    scale=crow[0:1, 4:5]), waits=[e_lnv], inc=True)
                set_br("B0", e_mean)
                set_br("B1", e_var)
                e_stat_r = ev_max(e_mean, e_var)
                e_mb = em.op("pe", lambda p: p.matmul(
                    PA[:, 0:512], onesf[:], stat[0:1, 512:1024], start=True, stop=True),
                    waits=[e_mean, br("A0")], inc=True)
                e_rb = em.op("pe", lambda p: p.matmul(
                    PA[:, 512:1024], onesf[:], stat[0:1, 1536:2048], start=True, stop=True),
                    waits=[e_rstd, br("A1")], inc=True)
                set_bw("A0", e_mb)
                set_bw("A1", e_rb)
                e_ln2ev = []
                _prev2 = None
                for c in range(NCH):
                    w1 = [e_mb, e_ln2_rd, _prev2]
                    if st == 0 and c == 0:
                        w1 += [e_wsl_guard[0], e_wsl_guard[1], e_wsl_guard[2],
                               e_wost_guard[0], e_ln2_guard]
                    esub2 = em.op("dve", lambda v, c=c, xb=xb: v.tensor_tensor(
                        stat[:, 0:512], x1b[xb][:, c * 512:(c + 1) * 512], PA[:, 0:512],
                        ALU.subtract), waits=w1, inc=True)
                    _prev2 = em.op("dve", lambda v, c=c: v.tensor_tensor(
                        ln2[:, c * 512:(c + 1) * 512], stat[:, 0:512], PA[:, 512:1024],
                        ALU.mult), waits=[e_rb, esub2], inc=True)
                    e_ln2ev.append(_prev2)
                e_ln2_all = ev_max(*e_ln2ev)
                set_br("A0", e_ln2_all)
                set_br("A1", e_ln2_all)
                e_x1_rd[xb] = ev_max(e_x1_rd[xb], e_ln2_all)

                if debug and s == 0 and st == 0:
                    em.op("sy", lambda g: g.dma_start(dbg["d_ln2"], ln2[:]),
                          waits=[e_ln2_all], inc=True)

                e_gelu = []
                for h4 in range(FH):
                    for g8 in range(NCH):
                        G = h4 * 8 + g8
                        use_b = (G % 2 == 1)
                        pref = "B" if use_b else "A"
                        e_mm = None
                        for t in range(4):
                            out_ap_fn = (lambda t: (lambda: PBb[t][:, 0:512])) if use_b \
                                else (lambda t: (lambda: PA[:, t * 512:(t + 1) * 512]))
                            e_mm = em.op("pe", lambda p, G=G, t=t, g8=g8, oo=out_ap_fn(t): p.matmul(
                                oo(),
                                rails[32 * t:32 * t + 32, G * 128:G * 128 + 128],
                                ln2[32 * t:32 * t + 32, g8 * 512:(g8 + 1) * 512],
                                start=True, stop=True, tile_position=(32 * t, 0)),
                                waits=[e_rails, e_ln2ev[g8], br("%s%d" % (pref, t))],
                                inc=(t == 3))
                        for t in range(4):
                            set_bw("%s%d" % (pref, t), e_mm)
                        e_ln2_rd = ev_max(e_ln2_rd, e_mm)
                        fc0 = h4 * 32 + 4 * g8
                        wg = [e_mm, e_cp_last]
                        if st == 0 and G == 0:
                            wg += [e_hT_last_read, e_spill[s], e_dbg_spill]
                        if use_b:
                            e_g = None
                            for t in range(4):
                                e_g1 = em.op("act", lambda a, fc0=fc0, t=t: a.activation(
                                    hT[:, (fc0 + t) * 512:(fc0 + t + 1) * 512],
                                    PBb[t][:, 0:512], AF.Gelu),
                                    waits=wg, inc=True)
                                set_br("B%d" % t, e_g1)
                                e_g = ev_max(e_g, e_g1)
                        else:
                            e_g = em.op("act", lambda a, fc0=fc0: a.activation(
                                hT[:, fc0 * 512:(fc0 + 4) * 512], PA[:, 0:2048], AF.Gelu),
                                waits=wg, inc=True)
                            for t in range(4):
                                set_br("A%d" % t, e_g)
                        e_gelu.append(e_g)
                e_gelu_all = ev_max(*e_gelu)

                if debug and s == 0 and st == 0:
                    em.op("sy", lambda g: g.dma_start(dbg["d_ht"], hT[:, 0:4096]),
                          waits=[e_gelu_all], inc=True)

                for ec in range(NCH):
                    cb = "B%d" % (ec % 2)
                    ccol = (ec % 2) * 512
                    e_mm = None
                    for fg in range(8):
                        half = fg % 2
                        wdma_w = [e_wcp[ec], wbuf_rd[half]]
                        if st == 0 and ec == 0 and fg < 2:
                            wdma_w += [e_exp_guard[0], e_exp_guard[1]]
                        e_wd = em.op("sy", lambda g, ec=ec, fg=fg, half=half: g.dma_start(
                            wbuf[:, half * 2048:(half + 1) * 2048],
                            wcpb[ec, :, fg * 16:(fg + 1) * 16, :].rearrange("p a b -> p (a b)")),
                            waits=wdma_w, inc=True)
                        for fc in range(fg * 16, fg * 16 + 16):
                            e_mm = em.op("pe", lambda p, fc=fc, ec=ec, half=half: p.matmul(
                                PBb[ec % 2][:, 0:512],
                                wbuf[:, half * 2048 + (fc % 16) * 128: half * 2048 + (fc % 16) * 128 + 128],
                                hT[:, fc * 512:(fc + 1) * 512],
                                start=(fc == 0), stop=(fc == 127)),
                                waits=[e_gelu_all, e_wd, br(cb) if fc == 0 else None],
                                inc=(fc % 16 == 15))
                        wbuf_rd[half] = ev_max(wbuf_rd[half], e_mm)
                    set_bw(cb, e_mm)
                    e_hT_last_read = ev_max(e_hT_last_read, e_mm)
                    e_cp_last = e_mm
                    e_ev2 = em.op("dve", lambda v, ec=ec, xb=xb: v.tensor_add(
                        outb[ec % 2][:], PBb[ec % 2][:, 0:512],
                        x1b[xb][:, ec * 512:(ec + 1) * 512]),
                        waits=[e_mm, e_x1ld[xb], e_outb_rd[ec % 2]], inc=True)
                    set_br(cb, e_ev2)
                    e_x1_rd[xb] = ev_max(e_x1_rd[xb], e_ev2)
                    e_st2 = em.op("sy", lambda g, ec=ec, st=st, s=s: g.dma_start(
                        y[s, ec * 128:(ec + 1) * 128, st * 512:(st + 1) * 512],
                        outb[ec % 2][:]), waits=[e_ev2], inc=True)
                    e_outb_rd[ec % 2] = e_st2
                    e_store_last = e_st2

            e_x1A_readers = e_x1_rd[0]
            e_x1B_readers = e_x1_rd[1]
            e_exp_guard = [ev_max(wbuf_rd[0], wbuf_rd[1])] * 2
            e_avnregion_guard = ev_max(e_outb_rd[0], e_outb_rd[1])
            e_lnbuf_guard = e_hT_last_read
            e_avt_guard = e_hT_last_read
            e_ln2_guard = e_ln2_rd
            e_wsl_guard = [e_ln2_rd, e_ln2_rd, e_ln2_rd]
            e_wost_guard = [e_ln2_rd,
                            ev_max(e_sx, e_sq, e_x1_rd[0], e_x1_rd[1])]

        pass
    except _Stop:
        pass
    em.op("sy", lambda g: None,
          waits=[{("sy%d" % i): em.scnt["sy%d" % i]} for i in range(NPOOL)
                 if em.scnt["sy%d" % i] > 0])

    from contextlib import ExitStack
    stack = ExitStack()
    sems = {sname: stack.enter_context(nc.semaphore("sem_" + sname))
            for sname in em.sem_names}
    block = stack.enter_context(nc.Block())

    def realize(eng_name):
        def section(eng):
            water = {}
            for fn, waits, inc_sem in em.ops[eng_name]:
                for tgt in sorted(waits):
                    val = waits[tgt]
                    if water.get(tgt, 0) >= val:
                        continue
                    eng.wait_ge(sems[tgt], val)
                    water[tgt] = val
                ins = fn(eng)
                if inc_sem is not None:
                    if ins is None:
                        raise RuntimeError("inc op returned no instruction")
                    ins.then_inc(sems[inc_sem], 16 if eng_name in ("gp", "sy") else 1)
        return section

    block.tensor(realize("pe"))
    block.scalar(realize("act"))
    block.vector(realize("dve"))
    block.gpsimd(realize("gp"))
    block.sync(realize("sy"))
    stack.close()
    return nc


_NC_CACHE = {}


def _get_nc(debug=False):
    key = bool(debug)
    if key not in _NC_CACHE:
        _NC_CACHE[key] = build_program(debug=key)
    return _NC_CACHE[key]


def _host_layouts(w_q, w_k, w_v, w_o, rail_maker, rail_up_proj, w_cproj):
    wq_h = np.ascontiguousarray(np.asarray(w_q, dtype=np.float32).reshape(E, E))
    wk_h = np.ascontiguousarray(np.asarray(w_k, dtype=np.float32).reshape(E, E))
    wv_h = np.ascontiguousarray(np.asarray(w_v, dtype=np.float32).reshape(E, E))
    woT_h = np.ascontiguousarray(np.asarray(w_o, dtype=np.float32).reshape(E, E).T)
    wcpT = np.asarray(w_cproj, dtype=np.float32).T
    wcp_h = np.ascontiguousarray(
        wcpT.reshape(128, 128, 8, 128).transpose(2, 1, 0, 3))
    mk = np.asarray(rail_maker, dtype=np.float32)
    up = np.asarray(rail_up_proj, dtype=np.float32)
    mkT_h = np.zeros((128, 1024), dtype=np.float32)
    upT_h = np.zeros((128, 4096), dtype=np.float32)
    for G in range(NG):
        h4, g8 = divmod(G, 8)
        for t in range(4):
            rp = 4 * g8 + t
            r0, r1 = 2 * rp, 2 * rp + 1
            mkT_h[32 * t:32 * t + 16, 32 * G:32 * G + 16] = mk[h4, r0].T
            mkT_h[32 * t + 16:32 * t + 32, 32 * G + 16:32 * G + 32] = mk[h4, r1].T
            upT_h[32 * t:32 * t + 16, 128 * G:128 * G + 64] = up[h4, r0].T
            upT_h[32 * t + 16:32 * t + 32, 128 * G + 64:128 * G + 128] = up[h4, r1].T
    return wq_h, wk_h, wv_h, woT_h, wcp_h, mkT_h, upT_h


def kernel(L0_attn_x_BSE, L0_attn_w_q, L0_attn_w_k, L0_attn_w_v, L0_attn_w_o,
           L0_ffn_rail_maker, L0_ffn_rail_up_proj, L0_ffn_w_cproj):
    x = np.asarray(L0_attn_x_BSE, dtype=np.float32)
    xt_full = np.ascontiguousarray(x.transpose(0, 2, 1))
    wq_h, wk_h, wv_h, woT_h, wcp_h, mkT_h, upT_h = _host_layouts(
        L0_attn_w_q, L0_attn_w_k, L0_attn_w_v, L0_attn_w_o,
        L0_ffn_rail_maker, L0_ffn_rail_up_proj, L0_ffn_w_cproj)
    nc = _get_nc(debug=False)
    in_maps = []
    for c in range(8):
        in_maps.append({
            "xt": np.ascontiguousarray(xt_full[2 * c:2 * c + 2]),
            "wq": wq_h, "wk": wk_h, "wv": wv_h, "woT": woT_h,
            "wcp": wcp_h, "mkT": mkT_h, "upT": upT_h,
        })
    res = run_bass_kernel_spmd(nc, in_maps, core_ids=list(range(8)))
    shards = [np.asarray(r["y"]) for r in res.results]
    out = np.concatenate(shards, axis=0)
    return np.ascontiguousarray(out.transpose(0, 2, 1)).astype(np.float32)


# version 04 (final)
# speedup vs baseline: 1.0298x; 1.0298x over previous
import numpy as np
import ml_dtypes
import concourse.bass as bass
import concourse.mybir as mybir
from concourse.bass_utils import run_bass_kernel_spmd

F32 = mybir.dt.float32
BF16 = mybir.dt.bfloat16
AF = mybir.ActivationFunctionType
ALU = mybir.AluOpType
AX = mybir.AxisListType

S = 2048
E = 1024
NCH = 8
NPAIR = 8
NKT = 16
NSPAN = 4
SUB = 512
NSUB = 4
FH = 4
NG = 32
LN8 = 2.0794415416798357
MSHIFT = 60.0

SB = 16512

O_XBUF = 0
O_LNBUF = 65536
O_AVT = 98304
O_RAILS = 131072
O_STAT = 139264
O_QA0 = 147456
O_QA1 = 151552
O_KA0 = 155648
O_KA1 = 159744
O_VAUG = 163840
O_SQT = 168448
O_EXP0 = 180224
O_EXP1 = 184320
O_AVN = 188416
O_WSL = 192512
O_CONST = 208896

ENGS = ("pe", "act", "dve", "gp", "sy")
NPOOL = 8


class EM:

    def __init__(self):
        self.ops = {e: [] for e in ENGS}
        self.cnt = {e: 0 for e in ENGS}
        self.sem_names = ["pe", "act", "dve"] + \
            ["%s%d" % (q, i) for q in ("gp", "sy") for i in range(NPOOL)]
        self.scnt = {sname: 0 for sname in self.sem_names}

    def op(self, eng, fn, waits=(), inc=False):
        w = {}
        for ev in waits:
            if ev is None:
                continue
            for k, v in ev.items():
                assert v <= self.scnt[k], "wait on future event"
                if w.get(k, -1) < v:
                    w[k] = v
        evt = None
        if inc:
            if eng in ("gp", "sy"):
                i = self.cnt[eng]
                self.cnt[eng] += 1
                sname = "%s%d" % (eng, i % NPOOL)
                if i >= NPOOL:
                    prev = 16 * (i // NPOOL)
                    if w.get(sname, -1) < prev:
                        w[sname] = prev
                self.scnt[sname] += 16
                evt = {sname: self.scnt[sname]}
                self.ops[eng].append((fn, w, sname))
                return evt
            else:
                self.cnt[eng] += 1
                self.scnt[eng] += 1
                evt = {eng: self.scnt[eng]}
                self.ops[eng].append((fn, w, eng))
                return evt
        self.ops[eng].append((fn, w, None))
        return evt


def ev_max(*evs):
    out = {}
    for ev in evs:
        if ev is None:
            continue
        for k, v in ev.items():
            if out.get(k, -1) < v:
                out[k] = v
    return out or None


def build_program(debug=False, trunc=None):
    nc = bass.Bass("TRN2", target_bir_lowering=False)

    xt = nc.dram_tensor("xt", [2, E, S], F32, kind="ExternalInput").ap()
    wq = nc.dram_tensor("wq", [E, E], F32, kind="ExternalInput").ap()
    wk = nc.dram_tensor("wk", [E, E], F32, kind="ExternalInput").ap()
    wv = nc.dram_tensor("wv", [E, E], F32, kind="ExternalInput").ap()
    woT = nc.dram_tensor("woT", [E, E], F32, kind="ExternalInput").ap()
    wcp = nc.dram_tensor("wcp", [8, 128, 128, 128], F32, kind="ExternalInput").ap()
    mkT = nc.dram_tensor("mkT", [128, 1024], F32, kind="ExternalInput").ap()
    upT = nc.dram_tensor("upT", [128, 4096], F32, kind="ExternalInput").ap()
    y = nc.dram_tensor("y", [2, E, S], F32, kind="ExternalOutput").ap()
    wcpb = nc.dram_tensor("wcpb", [8, 128, 128, 128], BF16, kind="Internal").ap()
    xsp = nc.dram_tensor("xsp", [2, E, S], F32, kind="Internal").ap()

    dbg = {}
    if debug:
        for name, shape, dt in [
            ("d_ln1", [128, 16384], BF16), ("d_qa0", [128, 2048], BF16),
            ("d_ka0", [128, 2048], BF16), ("d_qa1", [128, 2048], BF16),
            ("d_ka1", [128, 2048], BF16),
            ("d_vaug", [128, 2080], BF16), ("d_avt", [128, 2048], BF16),
            ("d_x1", [E, S], F32), ("d_exp", [128, 128], BF16),
            ("d_ln2", [128, 4096], BF16), ("d_ht", [128, 4096], BF16),
            ("d_avn", [128, 512], F32), ("d_rc", [128, 2048], F32),
            ("d_avpsum", [128, 512], F32), ("d_exp0", [128, 2048], BF16),
        ]:
            dbg[name] = nc.dram_tensor(name, shape, dt, kind="ExternalOutput").ap()

    ident_np = np.eye(128, dtype=np.float32)
    mask_np = np.zeros((128, 128), dtype=ml_dtypes.bfloat16)
    for k in range(128):
        mask_np[k, k:] = 1.0
    ones128_np = np.ones((128, 1), dtype=ml_dtypes.bfloat16)
    ind0_np = np.zeros((128, 65), dtype=ml_dtypes.bfloat16)
    ind0_np[0:64, 64] = 1.0
    ind1_np = np.zeros((128, 65), dtype=ml_dtypes.bfloat16)
    ind1_np[64:128, 64] = 1.0
    onesf_np = np.ones((1, 128), dtype=np.float32)
    h_identf = nc.inline_tensor(ident_np, name="c_identf")
    h_mask = nc.inline_tensor(np.asarray(mask_np), name="c_mask")
    h_ones = nc.inline_tensor(np.asarray(ones128_np), name="c_ones")
    h_ind0 = nc.inline_tensor(np.asarray(ind0_np), name="c_ind0")
    h_ind1 = nc.inline_tensor(np.asarray(ind1_np), name="c_ind1")
    h_onesf = nc.inline_tensor(onesf_np, name="c_onesf")

    def sb(name, shape, dt, off):
        return nc.alloc_sbuf_tensor_at(name, shape, dt, offset=SB + off)

    xbuf = sb("xbuf", [128, 8 * 2048], F32, O_XBUF)
    lnbuf = sb("lnbuf", [128, 8 * 2048], BF16, O_LNBUF)
    avt = sb("avt", [128, 8 * 2048], BF16, O_AVT)
    hT = sb("hT", [128, 65536], BF16, O_XBUF)
    rails = sb("rails", [128, 4096], BF16, O_RAILS)
    stat = sb("stat", [128, 2048], F32, O_STAT)
    qa = [sb("qa0", [128, 2048], BF16, O_QA0), sb("qa1", [128, 2048], BF16, O_QA1)]
    ka = [sb("ka0", [128, 2048], BF16, O_KA0), sb("ka1", [128, 2048], BF16, O_KA1)]
    vaug = sb("vaug", [128, 2080], BF16, O_VAUG)
    sqt = sb("sqt", [128, 2048], BF16, O_SQT)
    xball = sb("xball", [128, 4096], BF16, O_SQT)
    expS = [sb("expS0", [128, 2048], BF16, O_EXP0), sb("expS1", [128, 2048], BF16, O_EXP1)]
    avn = sb("avn", [128, 512], F32, O_AVN)
    lnsq = sb("lnsq", [128, 1024], BF16, O_AVN + 2048)
    wslq = sb("wslq", [128, 1024], BF16, O_WSL)
    wslk = sb("wslk", [128, 1024], BF16, O_WSL + 2048)
    wslv = sb("wslv", [128, 1024], BF16, O_WSL + 4096)
    wost = [sb("wost0", [128, 1024], BF16, O_WSL + 6144),
            sb("wost1", [128, 1024], BF16, O_WSL + 8192)]
    x1b = [sb("x1A", [128, 4096], F32, O_QA0), sb("x1B", [128, 4096], F32, O_VAUG)]
    wbuf = sb("wbuf", [128, 4096], BF16, O_EXP0)
    outb = [sb("outA", [128, 512], F32, O_AVN), sb("outB", [128, 512], F32, O_AVN + 2048)]
    ln2 = sb("ln2", [128, 4096], BF16, O_WSL)
    xb512 = sb("xb512", [128, 512], BF16, O_WSL + 8192)
    xq512 = sb("xq512", [128, 512], BF16, O_WSL + 9216)
    makbuf = sb("makbuf", [128, 1024], BF16, O_EXP0)
    upbuf = sb("upbuf", [128, 4096], BF16, O_SQT)
    masktri = sb("masktri", [128, 128], BF16, O_CONST)
    ones1 = sb("ones1", [128, 1], BF16, O_CONST + 256)
    ind0 = sb("ind0", [128, 65], BF16, O_CONST + 288)
    ind1 = sb("ind1", [128, 65], BF16, O_CONST + 448)
    onesf = sb("onesf", [1, 128], F32, O_CONST + 608)
    crow = sb("crow", [128, 32], F32, O_CONST + 1120)
    identf = sb("identf", [128, 128], F32, O_CONST + 1248)

    PA = nc.alloc_psum_tensor("PA", [128, 2048], F32)
    PBb = [nc.alloc_psum_tensor("PB%d" % i, [128, 512], F32) for i in range(4)]

    em = EM()

    bank_w = {}
    bank_r = {}

    def bw(b):
        return bank_w.get(b)

    def br(b):
        return bank_r.get(b)

    def set_bw(b, ev):
        bank_w[b] = ev_max(bank_w.get(b), ev)

    def set_br(b, ev):
        bank_r[b] = ev_max(bank_r.get(b), ev)

    class _Stop(Exception):
        pass

    def _ck(name):
        if trunc == name:
            raise _Stop()

    try:
        e_c = []
        for dst, src in [(masktri, h_mask), (ones1, h_ones), (ind0, h_ind0),
                         (ind1, h_ind1), (onesf, h_onesf), (identf, h_identf)]:
            e_c.append(em.op("gp", lambda g, d=dst, s=src: g.dma_start(d[:], s.ap()), inc=True))
        e_consts = ev_max(*e_c)

        e_cr = []
        for col, val in [(0, 1.0 / 1024.0), (1, 1e-5), (2, 0.125), (4, -0.5), (5, 0.5)]:
            e_cr.append(em.op("dve", lambda v, c=col, x=val: v.memset(crow[:, c:c + 1], x), inc=True))
        e_crow = ev_max(*e_cr)


        e_mk = em.op("gp", lambda g: g.dma_start(makbuf[:], mkT), inc=True)
        e_up = em.op("gp", lambda g: g.dma_start(upbuf[:], upT), inc=True)

        prev_evac = {}
        rails_ev = []
        for G in range(NG):
            slot = G % 4
            col = slot * 512
            e_mm = None
            for t in range(4):
                e_mm = em.op("pe", lambda p, t=t, col=col, G=G: p.matmul(
                    PA[32 * t:32 * t + 32, col:col + 128],
                    makbuf[32 * t:32 * t + 32, 32 * G:32 * G + 32],
                    upbuf[32 * t:32 * t + 32, 128 * G:128 * G + 128],
                    start=True, stop=True, tile_position=(32 * t, 32 * t)),
                    waits=[e_mk, e_up, prev_evac.get(slot)], inc=(t == 3))
            e_ev = em.op("act", lambda a, col=col, G=G: a.copy(
                rails[:, G * 128:(G + 1) * 128], PA[:, col:col + 128]),
                waits=[e_mm], inc=True)
            prev_evac[slot] = e_ev
            rails_ev.append(e_ev)
        e_rails = ev_max(*rails_ev)
        e_mcomb_pe = em.cnt["pe"]
        e_mcomb = {"pe": e_mcomb_pe}
        if debug:
            for tz in (qa[0], qa[1], ka[0], ka[1], stat, avn, vaug,
                       expS[0], expS[1]):
                e_crow = ev_max(e_crow, em.op(
                    "dve", lambda v, t=tz: v.memset(t[:], 0.0),
                    waits=[e_mcomb, e_crow], inc=True))
        for b in range(4):
            set_br("A%d" % b, e_rails)

        wcp_pending = list(range(8))
        e_wcp = [None] * 8

        def emit_wcp_cast():
            if wcp_pending:
                ec = wcp_pending.pop(0)
                e_wcp[ec] = em.op("gp", lambda g, ec=ec: g.dma_start(wcpb[ec], wcp[ec]), inc=True)

        e_hT_last_read = None
        e_spill = [None, None]
        e_lnbuf_guard = None
        e_avt_guard = None
        e_x1A_readers = None
        e_x1B_readers = None
        e_exp_guard = [None, None]
        e_avnregion_guard = None
        e_wsl_guard = [None, None, None]
        e_wost_guard = [None, None]
        e_ln2_guard = None
        e_stat_r = None
        e_store_last = None

        for s in range(2):
            e_xld = []
            for c in range(NCH):
                w = [e_hT_last_read, e_spill[s - 1] if s == 1 else None]
                e_xld.append(em.op("sy", lambda g, c=c, s=s: g.dma_start(
                    xbuf[:, c * 2048:(c + 1) * 2048], xt[s, 128 * c:128 * (c + 1), :]),
                    waits=w, inc=True))

            e_ms = []
            e_ms.append(em.op("dve", lambda v: v.memset(ka[0][64:65, :], 1.0),
                              waits=[e_x1A_readers, e_crow], inc=True))
            e_ms.append(em.op("dve", lambda v: v.memset(ka[1][64:65, :], 1.0),
                              waits=[e_x1A_readers], inc=True))
            vv = vaug.ap().rearrange("p (t c) -> p t c", c=130)
            e_ms.append(em.op("dve", lambda v, a=vv: v.memset(a[:, :, 64:65], 1.0),
                              waits=[e_x1B_readers], inc=True))
            e_ms.append(em.op("dve", lambda v, a=vv: v.memset(a[:, :, 129:130], 1.0),
                              waits=[e_x1B_readers], inc=True))
            e_kaug_init = ev_max(*e_ms)

            e_ln_done = {}
            e_xball_r = None
            for sp in range(NSPAN):
                e_sx = None
                e_sq = None
                sqev = {}
                for c in range(NCH):
                    e1 = em.op("dve", lambda v, c=c, sp=sp: v.tensor_copy(
                        xball[:, c * 512:(c + 1) * 512],
                        xbuf[:, c * 2048 + sp * 512: c * 2048 + sp * 512 + 512]),
                        waits=[e_xld[c], e_xball_r, e_mcomb, e_consts, e_crow,
                               e_x1B_readers], inc=True)
                    e2 = em.op("dve", lambda v, c=c: v.tensor_mul(
                        lnsq[:, (c % 2) * 512:(c % 2) * 512 + 512],
                        xball[:, c * 512:(c + 1) * 512],
                        xball[:, c * 512:(c + 1) * 512]),
                        waits=[e1, e_avnregion_guard, sqev.get(c - 2)], inc=True)
                    e_sx = em.op("pe", lambda p, c=c: p.matmul(
                        PBb[0][0:1, 0:512], ones1[:], xball[:, c * 512:(c + 1) * 512],
                        start=(c == 0), stop=(c == NCH - 1)),
                        waits=[e1, br("B0")], inc=True)
                    e_sq = em.op("pe", lambda p, c=c: p.matmul(
                        PBb[1][0:1, 0:512], ones1[:], lnsq[:, (c % 2) * 512:(c % 2) * 512 + 512],
                        start=(c == 0), stop=(c == NCH - 1)),
                        waits=[e2, br("B1")], inc=True)
                    sqev[c] = e_sq
                set_bw("B0", e_sx)
                set_bw("B1", e_sq)
                e_xball_r = ev_max(e_sx, e_sq)
                e_mean = em.op("act", lambda a: a.activation(
                    stat[0:1, 512:1024], PBb[0][0:1, 0:512], AF.Copy, scale=crow[0:1, 0:1]),
                    waits=[e_sx, e_stat_r], inc=True)
                e_m2 = em.op("dve", lambda v: v.tensor_mul(
                    stat[0:1, 1024:1536], stat[0:1, 512:1024], stat[0:1, 512:1024]),
                    waits=[e_mean], inc=True)
                e_var = em.op("dve", lambda v: v.scalar_tensor_tensor(
                    stat[0:1, 1536:2048], PBb[1][0:1, 0:512], 1.0 / 1024.0,
                    stat[0:1, 1024:1536], ALU.mult, ALU.subtract),
                    waits=[e_sq, e_m2], inc=True)
                e_lnv = em.op("act", lambda a: a.activation(
                    stat[0:1, 1024:1536], stat[0:1, 1536:2048], AF.Ln,
                    bias=crow[0:1, 1:2]), waits=[e_var], inc=True)
                e_rstd = em.op("act", lambda a: a.activation(
                    stat[0:1, 1536:2048], stat[0:1, 1024:1536], AF.Exp,
                    scale=crow[0:1, 4:5]), waits=[e_lnv], inc=True)
                set_br("B0", e_mean)
                set_br("B1", e_var)
                e_stat_r = ev_max(e_mean, e_var)
                e_mb = em.op("pe", lambda p: p.matmul(
                    PA[:, 0:512], onesf[:], stat[0:1, 512:1024], start=True, stop=True),
                    waits=[e_mean, br("A0")], inc=True)
                e_rb = em.op("pe", lambda p: p.matmul(
                    PA[:, 512:1024], onesf[:], stat[0:1, 1536:2048], start=True, stop=True),
                    waits=[e_rstd, br("A1")], inc=True)
                set_bw("A0", e_mb)
                set_bw("A1", e_rb)
                last = None
                for c in range(NCH):
                    esub = em.op("dve", lambda v, c=c, sp=sp: v.tensor_tensor(
                        stat[:, 0:512], xbuf[:, c * 2048 + sp * 512: c * 2048 + sp * 512 + 512],
                        PA[:, 0:512], ALU.subtract),
                        waits=[e_mb, e_lnbuf_guard, last], inc=True)
                    last = em.op("dve", lambda v, c=c, sp=sp: v.tensor_tensor(
                        lnbuf[:, c * 2048 + sp * 512: c * 2048 + sp * 512 + 512],
                        stat[:, 0:512], PA[:, 512:1024], ALU.mult),
                        waits=[e_rb, esub], inc=True)
                    e_ln_done[(sp, c)] = last
                set_br("A0", last)
                set_br("A1", last)
            e_ln_all = ev_max(*[e_ln_done[k] for k in e_ln_done])
            e_lnbuf_guard = None

            if debug and s == 0:
                em.op("sy", lambda g: g.dma_start(dbg["d_ln1"], lnbuf[:]),
                      waits=[e_ln_all], inc=True)
            _ck("ln1")

            wq_r = wq.rearrange("(c p) n -> p c n", p=128)
            wk_r = wk.rearrange("(c p) n -> p c n", p=128)
            wv_r = wv.rearrange("(c p) n -> p c n", p=128)
            woT_r = woT.rearrange("(c p) n -> p c n", p=128)

            def load_wsl(p_idx):
                res = []
                for i, (buf, src) in enumerate([(wslq, wq_r), (wslk, wk_r), (wslv, wv_r)]):
                    res.append(em.op("gp", lambda g, b=buf, sr=src, p=p_idx: g.dma_start(
                        b[:], sr[:, :, p * 128:(p + 1) * 128]),
                        waits=[e_wsl_guard[i], e_ln2_guard], inc=True))
                return res

            e_wsl = load_wsl(0)
            emit_wcp_cast()

            sc_toggle = [0]
            pend_tr = []
            e_avt_ev = None
            e_qa_rd = [e_x1A_readers, e_x1A_readers]
            e_ka_rd = [e_x1A_readers, e_x1A_readers]
            e_sqt_rd = e_x1B_readers
            e_avn_rd = e_avnregion_guard

            for p_idx in range(NPAIR):
                e_kaug_ev = [{}, {}]
                e_ksq = [None] * NSPAN
                for sp in range(NSPAN):
                    bk = "A%d" % (sp % 2)
                    col = (sp % 2) * 512
                    e_mm = None
                    for c in range(NCH):
                        e_mm = em.op("pe", lambda p, c=c, sp=sp, col=col: p.matmul(
                            PA[:, col:col + 512], wslk[:, c * 128:(c + 1) * 128],
                            lnbuf[:, c * 2048 + sp * 512: c * 2048 + sp * 512 + 512],
                            start=(c == 0), stop=(c == NCH - 1)),
                            waits=[e_wsl[1], e_ln_done[(sp, c)], br(bk), e_kaug_init],
                            inc=(c == NCH - 1))
                    set_bw(bk, e_mm)
                    e_wsl_guard[1] = ev_max(e_wsl_guard[1], e_mm)
                    ev0 = em.op("act", lambda a, sp=sp, col=col: a.activation(
                        ka[0][0:64, sp * 512:(sp + 1) * 512], PA[0:64, col:col + 512],
                        AF.Copy, scale=crow[0:64, 2:3]),
                        waits=[e_mm, e_ka_rd[0]], inc=True)
                    ev1 = em.op("act", lambda a, sp=sp, col=col: a.activation(
                        expS[1][64:128, sp * 512:(sp + 1) * 512], PA[64:128, col:col + 512],
                        AF.Copy, scale=crow[64:128, 2:3]),
                        waits=[e_mm, e_exp_guard[1]], inc=True)
                    e_kaug_ev[0][sp] = ev0
                    e_kaug_ev[1][sp] = ev1
                    esq = em.op("act", lambda a, sp=sp, col=col: a.activation(
                        sqt[:, sp * 512:(sp + 1) * 512], PA[:, col:col + 512], AF.Square),
                        waits=[e_mm, e_sqt_rd], inc=True)
                    e_ksq[sp] = esq
                    set_br(bk, ev_max(ev0, ev1, esq))
                e_sqt_rd = None
                if p_idx == 0:
                    _ck("kproj")
                e_shiftK = em.op("gp", lambda g: g.dma_start(
                    ka[1][0:64, :], expS[1][64:128, :]),
                    waits=[ev_max(*[e_kaug_ev[1][x] for x in range(NSPAN)]), e_ka_rd[1]],
                    inc=True)
                e_kmax = [None, None]
                for sp in range(NSPAN):
                    e_st0 = em.op("pe", lambda p, sp=sp: p.matmul(
                        PA[0:65, 1024:1536], ind0[:], sqt[:, sp * 512:(sp + 1) * 512],
                        start=True, stop=True), waits=[e_ksq[sp], br("A2")], inc=True)
                    set_bw("A2", e_st0)
                    e_st1 = em.op("pe", lambda p, sp=sp: p.matmul(
                        PA[0:65, 1536:2048], ind1[:], sqt[:, sp * 512:(sp + 1) * 512],
                        start=True, stop=True), waits=[e_ksq[sp], br("A3")], inc=True)
                    set_bw("A3", e_st1)
                    em0 = em.op("dve", lambda v, sp=sp: v.reduce_max(
                        stat[64:65, 1030 + sp:1031 + sp], PA[64:65, 1024:1536], axis=AX.X),
                        waits=[e_st0], inc=True)
                    em1 = em.op("dve", lambda v, sp=sp: v.reduce_max(
                        stat[64:65, 1050 + sp:1051 + sp], PA[64:65, 1536:2048], axis=AX.X),
                        waits=[e_st1], inc=True)
                    e_kmax[0] = ev_max(e_kmax[0], em0)
                    e_kmax[1] = ev_max(e_kmax[1], em1)
                    set_br("A2", em0)
                    set_br("A3", em1)
                e_mx0 = em.op("dve", lambda v: v.reduce_max(
                    stat[64:65, 1040:1041], stat[64:65, 1030:1034], axis=AX.X),
                    waits=[e_kmax[0]], inc=True)
                e_mx1 = em.op("dve", lambda v: v.reduce_max(
                    stat[64:65, 1055:1056], stat[64:65, 1050:1054], axis=AX.X),
                    waits=[e_kmax[1]], inc=True)
                e_l0 = em.op("act", lambda a: a.activation(
                    stat[64:65, 1042:1043], stat[64:65, 1040:1041], AF.Ln),
                    waits=[e_mx0], inc=True)
                e_l1 = em.op("act", lambda a: a.activation(
                    stat[64:65, 1057:1058], stat[64:65, 1055:1056], AF.Ln),
                    waits=[e_mx1], inc=True)
                e_lnc0 = em.op("dve", lambda v: v.tensor_scalar(
                    stat[64:65, 1044:1045], stat[64:65, 1042:1043], 0.5, -LN8,
                    ALU.mult, ALU.add), waits=[e_l0], inc=True)
                e_lnc1 = em.op("dve", lambda v: v.tensor_scalar(
                    stat[64:65, 1059:1060], stat[64:65, 1057:1058], 0.5, -LN8,
                    ALU.mult, ALU.add), waits=[e_l1], inc=True)

                e_qaug_ev = [{}, {}]
                e_mrow = [{}, {}]
                for sp in range(NSPAN):
                    bk = "A%d" % (sp % 2)
                    col = (sp % 2) * 512
                    e_mm = None
                    for c in range(NCH):
                        e_mm = em.op("pe", lambda p, c=c, sp=sp, col=col: p.matmul(
                            PA[:, col:col + 512], wslq[:, c * 128:(c + 1) * 128],
                            lnbuf[:, c * 2048 + sp * 512: c * 2048 + sp * 512 + 512],
                            start=(c == 0), stop=(c == NCH - 1)),
                            waits=[e_wsl[0], br(bk)], inc=(c == NCH - 1))
                    set_bw(bk, e_mm)
                    e_wsl_guard[0] = ev_max(e_wsl_guard[0], e_mm)
                    ev0 = em.op("act", lambda a, sp=sp, col=col: a.copy(
                        qa[0][0:64, sp * 512:(sp + 1) * 512], PA[0:64, col:col + 512]),
                        waits=[e_mm, e_qa_rd[0]], inc=True)
                    ev1 = em.op("act", lambda a, sp=sp, col=col: a.copy(
                        expS[0][64:128, sp * 512:(sp + 1) * 512], PA[64:128, col:col + 512]),
                        waits=[e_mm, e_exp_guard[0]], inc=True)
                    e_qaug_ev[0][sp] = ev0
                    e_qaug_ev[1][sp] = ev1
                    esq = em.op("act", lambda a, sp=sp, col=col: a.activation(
                        sqt[:, sp * 512:(sp + 1) * 512], PA[:, col:col + 512], AF.Square),
                        waits=[e_mm], inc=True)
                    set_br(bk, ev_max(ev0, ev1, esq))
                    e_st0 = em.op("pe", lambda p, sp=sp: p.matmul(
                        PA[0:65, 1024:1536], ind0[:], sqt[:, sp * 512:(sp + 1) * 512],
                        start=True, stop=True), waits=[esq, br("A2")], inc=True)
                    set_bw("A2", e_st0)
                    e_st1 = em.op("pe", lambda p, sp=sp: p.matmul(
                        PA[0:65, 1536:2048], ind1[:], sqt[:, sp * 512:(sp + 1) * 512],
                        start=True, stop=True), waits=[esq, br("A3")], inc=True)
                    set_bw("A3", e_st1)
                    a0 = em.op("act", lambda a: a.activation(
                        stat[64:65, 0:512], PA[64:65, 1024:1536], AF.Ln),
                        waits=[e_st0, e_mrow[1].get(sp - 1)], inc=True)
                    set_br("A2", a0)
                    b0 = em.op("act", lambda a: a.activation(
                        stat[64:65, 512:1024], stat[64:65, 0:512], AF.Exp,
                        scale=crow[64:65, 5:6], bias=stat[64:65, 1044:1045]),
                        waits=[a0, e_lnc0], inc=True)
                    m0 = em.op("dve", lambda v, sp=sp: v.tensor_scalar(
                        qa[0][64:65, sp * 512:(sp + 1) * 512], stat[64:65, 512:1024],
                        MSHIFT, -1.0, ALU.subtract, ALU.mult),
                        waits=[b0, e_qa_rd[0]], inc=True)
                    a1 = em.op("act", lambda a: a.activation(
                        stat[64:65, 0:512], PA[64:65, 1536:2048], AF.Ln),
                        waits=[e_st1, m0], inc=True)
                    set_br("A3", a1)
                    b1 = em.op("act", lambda a: a.activation(
                        stat[64:65, 512:1024], stat[64:65, 0:512], AF.Exp,
                        scale=crow[64:65, 5:6], bias=stat[64:65, 1059:1060]),
                        waits=[a1, e_lnc1, m0], inc=True)
                    m1 = em.op("dve", lambda v, sp=sp: v.tensor_scalar(
                        qa[1][64:65, sp * 512:(sp + 1) * 512], stat[64:65, 512:1024],
                        MSHIFT, -1.0, ALU.subtract, ALU.mult),
                        waits=[b1, e_qa_rd[1]], inc=True)
                    e_mrow[0][sp] = m0
                    e_mrow[1][sp] = m1
                if p_idx == 0:
                    _ck("qproj")
                e_shiftQ = em.op("gp", lambda g: g.dma_start(
                    qa[1][0:64, :], expS[0][64:128, :]),
                    waits=[ev_max(*[e_qaug_ev[1][x] for x in range(NSPAN)]), e_qa_rd[1]],
                    inc=True)
                e_qaug_ev[1] = {sp: e_shiftQ for sp in range(NSPAN)}
                e_kaug_ev[1] = {x: e_shiftK for x in range(NSPAN)}
                e_exp_guard = [ev_max(e_exp_guard[0], e_shiftQ),
                               ev_max(e_exp_guard[1], e_shiftK)]
                e_qa_rd = [None, None]
                e_ka_rd = [None, None]

                e_vev = [None] * NKT
                for tt in range(NKT):
                    bk = "A%d" % (tt % 2)
                    col = (tt % 2) * 512
                    e_mm = None
                    for c in range(NCH):
                        e_mm = em.op("pe", lambda p, c=c, tt=tt, col=col: p.matmul(
                            PA[:, col:col + 128],
                            lnbuf[:, c * 2048 + tt * 128: c * 2048 + tt * 128 + 128],
                            wslv[:, c * 128:(c + 1) * 128],
                            start=(c == 0), stop=(c == NCH - 1)),
                            waits=[e_wsl[2], br(bk)], inc=(c == NCH - 1))
                    set_bw(bk, e_mm)
                    e_wsl_guard[2] = ev_max(e_wsl_guard[2], e_mm)
                    d0 = em.op("dve", lambda v, tt=tt, col=col: v.tensor_copy(
                        vaug[:, tt * 130:tt * 130 + 64], PA[:, col:col + 64]),
                        waits=[e_mm, e_kaug_init], inc=True)
                    d1 = em.op("dve", lambda v, tt=tt, col=col: v.tensor_copy(
                        vaug[:, tt * 130 + 65:tt * 130 + 129], PA[:, col + 64:col + 128]),
                        waits=[d0], inc=True)
                    e_vev[tt] = d1
                    set_br(bk, d1)

                if p_idx == 0:
                    _ck("vproj")
                if p_idx + 1 < NPAIR:
                    e_wsl = load_wsl(p_idx + 1)
                emit_wcp_cast()

                if debug and s == 0 and p_idx == 0:
                    em.op("sy", lambda g: g.dma_start(dbg["d_qa0"], qa[0][:]),
                          waits=[ev_max(*[e_qaug_ev[0][x] for x in range(4)]),
                                 ev_max(*[e_mrow[0][x] for x in range(4)])], inc=True)
                    em.op("sy", lambda g: g.dma_start(dbg["d_qa1"], qa[1][:]),
                          waits=[ev_max(*[e_qaug_ev[1][x] for x in range(4)]),
                                 ev_max(*[e_mrow[1][x] for x in range(4)])], inc=True)
                    em.op("sy", lambda g: g.dma_start(dbg["d_ka0"], ka[0][:]),
                          waits=[ev_max(*[e_kaug_ev[0][x] for x in range(4)]), e_kaug_init], inc=True)
                    em.op("sy", lambda g: g.dma_start(dbg["d_ka1"], ka[1][:]),
                          waits=[ev_max(*[e_kaug_ev[1][x] for x in range(4)]), e_kaug_init], inc=True)
                    em.op("sy", lambda g: g.dma_start(dbg["d_vaug"], vaug[:]),
                          waits=[ev_max(*e_vev)], inc=True)

                def drain_tr(item):
                    nonlocal e_avn_rd, e_avt_ev, e_avt_guard
                    b = item["b"]
                    tb = b - 1 if b > 0 else 0
                    thh = item["hh"]
                    tp = item["p"]
                    prow = thh * 64
                    e_tr = None
                    for qt in range(4 * b, 4 * b + 4):
                        e_tr = em.op("pe", lambda p, qt=qt, thh=thh, tb=tb: p.transpose(
                            PBb[tb][0:64 + thh * 64, (qt % 4) * 128:(qt % 4) * 128 + 128],
                            avn[:, (qt % 4) * 128:(qt % 4) * 128 + 64 + thh * 64],
                            identf[:]),
                            waits=[item["nrm"][qt], br("B%d" % tb)], inc=(qt % 4 == 3))
                    set_bw("B%d" % tb, e_tr)
                    e_ae = em.op("act", lambda a, tb=tb, b=b, tp=tp, prow=prow: a.copy(
                        avt[prow:prow + 64,
                            tp * 2048 + b * 512: tp * 2048 + b * 512 + 512],
                        PBb[tb][prow:prow + 64, 0:512]),
                        waits=[e_tr, e_avt_guard], inc=True)
                    set_br("B%d" % tb, e_ae)
                    e_avn_rd = ev_max(e_avn_rd, e_tr)
                    e_avt_ev = ev_max(e_avt_ev, e_ae)

                for hh in range(2):
                    q_t = qa[hh]
                    k_t = ka[hh]
                    rows = (0, 65)
                    e_exp_sp = {}
                    e_mask_kt = {}
                    e_bankdone = {}
                    e_nrm = {}
                    e_avlast = {}
                    e_dbg_stat = None
                    def emit_scores(kt):
                        q0 = kt * 128
                        nsp = -(-(S - q0) // 512)
                        for j in range(nsp):
                            c0 = q0 + j * 512
                            n = min(512, S - c0)
                            bsl = 2 + sc_toggle[0]
                            sc_toggle[0] ^= 1
                            bk = "A%d" % bsl
                            col = bsl * 512
                            spans = list(range(c0 // 512, (c0 + n - 1) // 512 + 1))
                            wts = [br(bk), e_kaug_ev[hh].get(kt // 4),
                                   ev_max(*[e_qaug_ev[hh][x] for x in spans]),
                                   ev_max(*[e_mrow[hh][x] for x in spans])]
                            e_mm = em.op("pe", lambda p, kt=kt, c0=c0, n=n, col=col,
                                         q_t=q_t, k_t=k_t, r=rows: p.matmul(
                                PA[:, col:col + n],
                                k_t[r[0]:r[1], kt * 128:kt * 128 + 128],
                                q_t[r[0]:r[1], c0:c0 + n],
                                start=True, stop=True),
                                waits=wts, inc=True)
                            set_bw(bk, e_mm)
                            e_ex = em.op("act", lambda a, kt=kt, c0=c0, n=n, col=col: a.activation(
                                expS[kt % 2][:, c0 - kt * 128:c0 - kt * 128 + n],
                                PA[:, col:col + n], AF.Exp),
                                waits=[e_mm, e_avlast.get(kt - 2),
                                       e_exp_guard[kt % 2]], inc=True)
                            e_exp_sp[(kt, j)] = e_ex
                            set_br(bk, e_ex)
                            if j == 0:
                                e_mask_kt[kt] = em.op("dve", lambda v, kt=kt: v.tensor_tensor(
                                    expS[kt % 2][:, 0:128], expS[kt % 2][:, 0:128],
                                    masktri[:], ALU.mult), waits=[e_ex], inc=True)

                    emit_scores(0)
                    for kt in range(NKT):
                        if kt + 1 < NKT:
                            emit_scores(kt + 1)
                        while pend_tr and pend_tr[0]["hseq"] < 2 * p_idx + hh:
                            drain_tr(pend_tr.pop(0))
                        last_ev = None
                        for qt in range(kt, NKT):
                            j = ((qt - kt) * 128) // 512
                            wts = [e_exp_sp[(kt, j)], e_vev[kt]]
                            if qt == kt:
                                wts.append(e_mask_kt[kt])
                            first = (kt == 0 and qt % 4 == 0)
                            if first:
                                wts.append(br("B%d" % (qt // 4)))
                            e_av = em.op("pe", lambda p, kt=kt, qt=qt, hh=hh: p.matmul(
                                PBb[qt // 4][:, (qt % 4) * 128:(qt % 4) * 128 + 65],
                                expS[kt % 2][:, (qt - kt) * 128:(qt - kt) * 128 + 128],
                                vaug[:, kt * 130 + hh * 65: kt * 130 + hh * 65 + 65],
                                start=(kt == 0 and qt % 4 == 0),
                                stop=(kt == qt and qt % 4 == 3)),
                                waits=wts, inc=True)
                            last_ev = e_av
                            if first:
                                set_bw("B%d" % (qt // 4), e_av)
                            if qt == kt:
                                e_bankdone[qt // 4] = ev_max(e_bankdone.get(qt // 4), e_av)
                        if debug and s == 0 and p_idx == 0 and hh == 0 and kt == 0:
                            edump = em.op("sy", lambda g: g.dma_start(
                                dbg["d_exp0"], expS[0][:]),
                                waits=[e_mask_kt[0], last_ev], inc=True)
                            last_ev = ev_max(last_ev, edump)
                        e_avlast[kt] = last_ev
                        e_exp_guard[kt % 2] = last_ev
                        if kt % 4 == 3:
                            b = kt // 4
                            if debug and s == 0 and p_idx == 0 and hh == 0 and b == 0:
                                e_cpy = None
                                for qt_ in range(4):
                                    e_cpy = em.op("dve", lambda v, qt_=qt_: v.tensor_copy(
                                        outb[1][:, qt_ * 128:qt_ * 128 + 65],
                                        PBb[0][:, qt_ * 128:qt_ * 128 + 65]),
                                        waits=[e_bankdone[0], e_cpy], inc=True)
                                for qt_ in range(4):
                                    em.op("sy", lambda g, qt_=qt_: g.dma_start(
                                        dbg["d_avpsum"][:, qt_ * 128:qt_ * 128 + 65],
                                        outb[1][:, qt_ * 128:qt_ * 128 + 65]),
                                        waits=[e_cpy], inc=True)
                            for qt in range(4 * b, 4 * b + 4):
                                rc = em.op("dve", lambda v, qt=qt: v.reciprocal(
                                    stat[:, 1536 + (qt % 4):1537 + (qt % 4)],
                                    PBb[qt // 4][:, (qt % 4) * 128 + 64:(qt % 4) * 128 + 65]),
                                    waits=[e_bankdone[b], e_dbg_stat], inc=True)
                                nr = em.op("dve", lambda v, qt=qt, hh=hh: v.tensor_single_scalar(
                                    avn[:, (qt % 4) * 128 + hh * 64:(qt % 4) * 128 + hh * 64 + 64],
                                    PBb[qt // 4][:, (qt % 4) * 128:(qt % 4) * 128 + 64],
                                    stat[:, 1536 + (qt % 4):1537 + (qt % 4)], ALU.mult),
                                    waits=[rc, e_avn_rd], inc=True)
                                e_nrm[qt] = nr
                            pend_tr.append({"b": b, "hh": hh, "p": p_idx, "hseq": 2 * p_idx + hh, "nrm": {qt: e_nrm[qt] for qt in range(4 * b, 4 * b + 4)}})
                            set_br("B%d" % b, ev_max(*[e_nrm[x] for x in range(4 * b, 4 * b + 4)]))
                            if debug and s == 0 and p_idx == 0 and hh == 0 and b == 0:
                                edmp = em.op("sy", lambda g: g.dma_start(
                                    dbg["d_avn"], avn[:]),
                                    waits=[e_nrm[3]], inc=True)
                                e_avn_rd = ev_max(e_avn_rd, edmp)
                                e_dbg_stat = em.op("sy", lambda g: g.dma_start(
                                    dbg["d_rc"], stat[:]), waits=[e_nrm[3]], inc=True)
                        if p_idx == 0 and hh == 0 and trunc == ("kt%d" % kt):
                            _ck("kt%d" % kt)
                        while pend_tr and (pend_tr[0]["hseq"] == 2 * p_idx + hh
                                           and kt >= 4 * pend_tr[0]["b"] + 5):
                            drain_tr(pend_tr.pop(0))
                            if p_idx == 0 and hh == 0 and trunc == ("tr%d" % kt):
                                _ck("tr%d" % kt)
                        if p_idx == 0 and hh == 0 and trunc == ("tr%d" % kt):
                            _ck("tr%d" % kt)
                    e_qa_rd[hh] = ev_max(e_qa_rd[hh], *[e for e in e_avlast.values() if e])
                    if p_idx == 0 and hh == 0:
                        _ck("head0")
                    if p_idx == 0 and hh == 1:
                        _ck("head1")
                    if p_idx == 1 and hh == 1:
                        _ck("pair1")
                    e_ka_rd[hh] = e_qa_rd[hh]
                    if debug and s == 0 and p_idx == 0 and hh == 0:
                        _ed = em.op("sy", lambda g: g.dma_start(dbg["d_exp"], expS[1][:, 0:128]),
                                    waits=[e_exp_sp[(15, 0)], e_mask_kt[15]], inc=True)
                        e_exp_guard[1] = ev_max(e_exp_guard[1], _ed)
            _ck("pairs")
            e_avt_guard = None
            e_avnregion_guard = None

            while pend_tr:
                drain_tr(pend_tr.pop(0))

            e_wo = [None, None]
            e_xupd = []
            for ec in range(NCH):
                e_wo[ec % 2] = em.op("gp", lambda g, ec=ec: g.dma_start(
                    wost[ec % 2][:], woT_r[:, :, ec * 128:(ec + 1) * 128]),
                    waits=[e_wost_guard[ec % 2]], inc=True)
                for sp in range(NSPAN):
                    bk = "A%d" % (sp % 2)
                    col = (sp % 2) * 512
                    e_mm = None
                    for hc in range(NCH):
                        e_mm = em.op("pe", lambda p, ec=ec, sp=sp, hc=hc, col=col: p.matmul(
                            PA[:, col:col + 512], wost[ec % 2][:, hc * 128:(hc + 1) * 128],
                            avt[:, hc * 2048 + sp * 512: hc * 2048 + sp * 512 + 512],
                            start=(hc == 0), stop=(hc == NCH - 1)),
                            waits=[e_wo[ec % 2], e_avt_ev, br(bk)], inc=(hc == NCH - 1))
                    set_bw(bk, e_mm)
                    e_wost_guard[ec % 2] = ev_max(e_wost_guard[ec % 2], e_mm)
                    e_up2 = em.op("dve", lambda v, ec=ec, sp=sp, col=col: v.tensor_add(
                        xbuf[:, ec * 2048 + sp * 512: ec * 2048 + sp * 512 + 512],
                        xbuf[:, ec * 2048 + sp * 512: ec * 2048 + sp * 512 + 512],
                        PA[:, col:col + 512]), waits=[e_mm], inc=True)
                    set_br(bk, e_up2)
                    e_xupd.append(e_up2)
            e_xupd_all = ev_max(*e_xupd)

            e_sp_list = []
            for c in range(NCH):
                e_sp_list.append(em.op("sy", lambda g, c=c, s=s: g.dma_start(
                    xsp[s, 128 * c:128 * (c + 1), :], xbuf[:, c * 2048:(c + 1) * 2048]),
                    waits=[e_xupd_all], inc=True))
            e_spill[s] = ev_max(*e_sp_list)

            e_dbg_spill = None
            if debug and s == 0:
                _d1 = em.op("sy", lambda g: g.dma_start(dbg["d_x1"], xsp[0]),
                            waits=[e_spill[s]], inc=True)
                _d2 = em.op("sy", lambda g: g.dma_start(dbg["d_avt"], avt[:, 0:2048]),
                            waits=[e_avt_ev], inc=True)
                e_dbg_spill = ev_max(_d1, _d2)
            _ck("oproj")

            e_cp_last = None
            e_x1ld = [None, None]
            e_x1_rd = [None, None]
            e_ln2_rd = None
            wbuf_rd = [None, None]
            e_outb_rd = [None, None]
            xsp_r = xsp[s].rearrange("(c p) t -> p c t", p=128)
            x1v = [x1b[0].ap().rearrange("p (c t) -> p c t", t=512),
                   x1b[1].ap().rearrange("p (c t) -> p c t", t=512)]
            e_gelu_all = None

            for st in range(NSUB):
                xb = st % 2
                e_x1ld[xb] = em.op("sy", lambda g, st=st, xb=xb, xsp_r=xsp_r: g.dma_start(
                    x1v[xb], xsp_r[:, :, st * 512:(st + 1) * 512]),
                    waits=[e_spill[s], e_x1_rd[xb]], inc=True)
                e_sx = None
                e_sq = None
                e_xb5 = None
                for c in range(NCH):
                    w5 = [e_x1ld[xb], e_sx, e_sq]
                    if st == 0 and c == 0:
                        w5 += [e_wost_guard[1], e_ln2_guard]
                    e_xb5 = em.op("dve", lambda v, c=c, xb=xb: v.tensor_copy(
                        xb512[:, 0:512], x1b[xb][:, c * 512:(c + 1) * 512]),
                        waits=w5, inc=True)
                    e_xq5 = em.op("dve", lambda v: v.tensor_mul(
                        xq512[:, 0:512], xb512[:, 0:512], xb512[:, 0:512]),
                        waits=[e_xb5], inc=True)
                    e_sx = em.op("pe", lambda p, c=c: p.matmul(
                        PBb[0][0:1, 0:512], ones1[:], xb512[:, 0:512],
                        start=(c == 0), stop=(c == NCH - 1)),
                        waits=[e_xb5, br("B0")], inc=True)
                    e_sq = em.op("pe", lambda p, c=c: p.matmul(
                        PBb[1][0:1, 0:512], ones1[:], xq512[:, 0:512],
                        start=(c == 0), stop=(c == NCH - 1)),
                        waits=[e_xq5, br("B1")], inc=True)
                set_bw("B0", e_sx)
                set_bw("B1", e_sq)
                e_mean = em.op("act", lambda a: a.activation(
                    stat[0:1, 512:1024], PBb[0][0:1, 0:512], AF.Copy, scale=crow[0:1, 0:1]),
                    waits=[e_sx, e_stat_r], inc=True)
                e_m2 = em.op("dve", lambda v: v.tensor_mul(
                    stat[0:1, 1024:1536], stat[0:1, 512:1024], stat[0:1, 512:1024]),
                    waits=[e_mean], inc=True)
                e_var = em.op("dve", lambda v: v.scalar_tensor_tensor(
                    stat[0:1, 1536:2048], PBb[1][0:1, 0:512], 1.0 / 1024.0,
                    stat[0:1, 1024:1536], ALU.mult, ALU.subtract),
                    waits=[e_sq, e_m2], inc=True)
                e_lnv = em.op("act", lambda a: a.activation(
                    stat[0:1, 1024:1536], stat[0:1, 1536:2048], AF.Ln,
                    bias=crow[0:1, 1:2]), waits=[e_var], inc=True)
                e_rstd = em.op("act", lambda a: a.activation(
                    stat[0:1, 1536:2048], stat[0:1, 1024:1536], AF.Exp,
                    scale=crow[0:1, 4:5]), waits=[e_lnv], inc=True)
                set_br("B0", e_mean)
                set_br("B1", e_var)
                e_stat_r = ev_max(e_mean, e_var)
                e_mb = em.op("pe", lambda p: p.matmul(
                    PA[:, 0:512], onesf[:], stat[0:1, 512:1024], start=True, stop=True),
                    waits=[e_mean, br("A0")], inc=True)
                e_rb = em.op("pe", lambda p: p.matmul(
                    PA[:, 512:1024], onesf[:], stat[0:1, 1536:2048], start=True, stop=True),
                    waits=[e_rstd, br("A1")], inc=True)
                set_bw("A0", e_mb)
                set_bw("A1", e_rb)
                e_ln2ev = []
                _prev2 = None
                for c in range(NCH):
                    w1 = [e_mb, e_ln2_rd, _prev2]
                    if st == 0 and c == 0:
                        w1 += [e_wsl_guard[0], e_wsl_guard[1], e_wsl_guard[2],
                               e_wost_guard[0], e_ln2_guard]
                    esub2 = em.op("dve", lambda v, c=c, xb=xb: v.tensor_tensor(
                        stat[:, 0:512], x1b[xb][:, c * 512:(c + 1) * 512], PA[:, 0:512],
                        ALU.subtract), waits=w1, inc=True)
                    _prev2 = em.op("dve", lambda v, c=c: v.tensor_tensor(
                        ln2[:, c * 512:(c + 1) * 512], stat[:, 0:512], PA[:, 512:1024],
                        ALU.mult), waits=[e_rb, esub2], inc=True)
                    e_ln2ev.append(_prev2)
                e_ln2_all = ev_max(*e_ln2ev)
                set_br("A0", e_ln2_all)
                set_br("A1", e_ln2_all)
                e_x1_rd[xb] = ev_max(e_x1_rd[xb], e_ln2_all)

                if debug and s == 0 and st == 0:
                    em.op("sy", lambda g: g.dma_start(dbg["d_ln2"], ln2[:]),
                          waits=[e_ln2_all], inc=True)

                e_gelu = []
                for h4 in range(FH):
                    for g8 in range(NCH):
                        G = h4 * 8 + g8
                        use_b = (G % 2 == 1)
                        pref = "B" if use_b else "A"
                        e_mm = None
                        for t in range(4):
                            out_ap_fn = (lambda t: (lambda: PBb[t][:, 0:512])) if use_b \
                                else (lambda t: (lambda: PA[:, t * 512:(t + 1) * 512]))
                            e_mm = em.op("pe", lambda p, G=G, t=t, g8=g8, oo=out_ap_fn(t): p.matmul(
                                oo(),
                                rails[32 * t:32 * t + 32, G * 128:G * 128 + 128],
                                ln2[32 * t:32 * t + 32, g8 * 512:(g8 + 1) * 512],
                                start=True, stop=True, tile_position=(32 * t, 0)),
                                waits=[e_rails, e_ln2ev[g8], br("%s%d" % (pref, t))],
                                inc=(t == 3))
                        for t in range(4):
                            set_bw("%s%d" % (pref, t), e_mm)
                        e_ln2_rd = ev_max(e_ln2_rd, e_mm)
                        fc0 = h4 * 32 + 4 * g8
                        wg = [e_mm, e_cp_last]
                        if st == 0 and G == 0:
                            wg += [e_hT_last_read, e_spill[s], e_dbg_spill]
                        if use_b:
                            e_g = None
                            for t in range(4):
                                e_g1 = em.op("act", lambda a, fc0=fc0, t=t: a.activation(
                                    hT[:, (fc0 + t) * 512:(fc0 + t + 1) * 512],
                                    PBb[t][:, 0:512], AF.Gelu),
                                    waits=wg, inc=True)
                                set_br("B%d" % t, e_g1)
                                e_g = ev_max(e_g, e_g1)
                        else:
                            e_g = em.op("act", lambda a, fc0=fc0: a.activation(
                                hT[:, fc0 * 512:(fc0 + 4) * 512], PA[:, 0:2048], AF.Gelu),
                                waits=wg, inc=True)
                            for t in range(4):
                                set_br("A%d" % t, e_g)
                        e_gelu.append(e_g)
                e_gelu_all = ev_max(*e_gelu)

                if debug and s == 0 and st == 0:
                    em.op("sy", lambda g: g.dma_start(dbg["d_ht"], hT[:, 0:4096]),
                          waits=[e_gelu_all], inc=True)

                for ec in range(NCH):
                    cb = "B%d" % (ec % 2)
                    ccol = (ec % 2) * 512
                    e_mm = None
                    for fg in range(8):
                        half = fg % 2
                        wdma_w = [e_wcp[ec], wbuf_rd[half]]
                        if st == 0 and ec == 0 and fg < 2:
                            wdma_w += [e_exp_guard[0], e_exp_guard[1]]
                        e_wd = em.op("sy", lambda g, ec=ec, fg=fg, half=half: g.dma_start(
                            wbuf[:, half * 2048:(half + 1) * 2048],
                            wcpb[ec, :, fg * 16:(fg + 1) * 16, :].rearrange("p a b -> p (a b)")),
                            waits=wdma_w, inc=True)
                        for fc in range(fg * 16, fg * 16 + 16):
                            e_mm = em.op("pe", lambda p, fc=fc, ec=ec, half=half: p.matmul(
                                PBb[ec % 2][:, 0:512],
                                wbuf[:, half * 2048 + (fc % 16) * 128: half * 2048 + (fc % 16) * 128 + 128],
                                hT[:, fc * 512:(fc + 1) * 512],
                                start=(fc == 0), stop=(fc == 127)),
                                waits=[e_gelu_all, e_wd, br(cb) if fc == 0 else None],
                                inc=(fc % 16 == 15))
                        wbuf_rd[half] = ev_max(wbuf_rd[half], e_mm)
                    set_bw(cb, e_mm)
                    e_hT_last_read = ev_max(e_hT_last_read, e_mm)
                    e_cp_last = e_mm
                    e_ev2 = em.op("dve", lambda v, ec=ec, xb=xb: v.tensor_add(
                        outb[ec % 2][:], PBb[ec % 2][:, 0:512],
                        x1b[xb][:, ec * 512:(ec + 1) * 512]),
                        waits=[e_mm, e_x1ld[xb], e_outb_rd[ec % 2]], inc=True)
                    set_br(cb, e_ev2)
                    e_x1_rd[xb] = ev_max(e_x1_rd[xb], e_ev2)
                    e_st2 = em.op("sy", lambda g, ec=ec, st=st, s=s: g.dma_start(
                        y[s, ec * 128:(ec + 1) * 128, st * 512:(st + 1) * 512],
                        outb[ec % 2][:]), waits=[e_ev2], inc=True)
                    e_outb_rd[ec % 2] = e_st2
                    e_store_last = e_st2

            e_x1A_readers = e_x1_rd[0]
            e_x1B_readers = e_x1_rd[1]
            e_exp_guard = [ev_max(wbuf_rd[0], wbuf_rd[1])] * 2
            e_avnregion_guard = ev_max(e_outb_rd[0], e_outb_rd[1])
            e_lnbuf_guard = e_hT_last_read
            e_avt_guard = e_hT_last_read
            e_ln2_guard = e_ln2_rd
            e_wsl_guard = [e_ln2_rd, e_ln2_rd, e_ln2_rd]
            e_wost_guard = [e_ln2_rd,
                            ev_max(e_sx, e_sq, e_x1_rd[0], e_x1_rd[1])]

        pass
    except _Stop:
        pass
    em.op("sy", lambda g: None,
          waits=[{("sy%d" % i): em.scnt["sy%d" % i]} for i in range(NPOOL)
                 if em.scnt["sy%d" % i] > 0])

    from contextlib import ExitStack
    stack = ExitStack()
    sems = {sname: stack.enter_context(nc.semaphore("sem_" + sname))
            for sname in em.sem_names}
    block = stack.enter_context(nc.Block())

    def realize(eng_name):
        def section(eng):
            water = {}
            for fn, waits, inc_sem in em.ops[eng_name]:
                for tgt in sorted(waits):
                    val = waits[tgt]
                    if water.get(tgt, 0) >= val:
                        continue
                    eng.wait_ge(sems[tgt], val)
                    water[tgt] = val
                ins = fn(eng)
                if inc_sem is not None:
                    if ins is None:
                        raise RuntimeError("inc op returned no instruction")
                    ins.then_inc(sems[inc_sem], 16 if eng_name in ("gp", "sy") else 1)
        return section

    block.tensor(realize("pe"))
    block.scalar(realize("act"))
    block.vector(realize("dve"))
    block.gpsimd(realize("gp"))
    block.sync(realize("sy"))
    stack.close()
    return nc


_NC_CACHE = {}


def _get_nc(debug=False):
    key = bool(debug)
    if key not in _NC_CACHE:
        _NC_CACHE[key] = build_program(debug=key)
    return _NC_CACHE[key]


def _host_layouts(w_q, w_k, w_v, w_o, rail_maker, rail_up_proj, w_cproj):
    wq_h = np.ascontiguousarray(np.asarray(w_q, dtype=np.float32).reshape(E, E))
    wk_h = np.ascontiguousarray(np.asarray(w_k, dtype=np.float32).reshape(E, E))
    wv_h = np.ascontiguousarray(np.asarray(w_v, dtype=np.float32).reshape(E, E))
    woT_h = np.ascontiguousarray(np.asarray(w_o, dtype=np.float32).reshape(E, E).T)
    wcpT = np.asarray(w_cproj, dtype=np.float32).T
    wcp_h = np.ascontiguousarray(
        wcpT.reshape(128, 128, 8, 128).transpose(2, 1, 0, 3))
    mk = np.asarray(rail_maker, dtype=np.float32)
    up = np.asarray(rail_up_proj, dtype=np.float32)
    mkT_h = np.zeros((128, 1024), dtype=np.float32)
    upT_h = np.zeros((128, 4096), dtype=np.float32)
    for G in range(NG):
        h4, g8 = divmod(G, 8)
        for t in range(4):
            rp = 4 * g8 + t
            r0, r1 = 2 * rp, 2 * rp + 1
            mkT_h[32 * t:32 * t + 16, 32 * G:32 * G + 16] = mk[h4, r0].T
            mkT_h[32 * t + 16:32 * t + 32, 32 * G + 16:32 * G + 32] = mk[h4, r1].T
            upT_h[32 * t:32 * t + 16, 128 * G:128 * G + 64] = up[h4, r0].T
            upT_h[32 * t + 16:32 * t + 32, 128 * G + 64:128 * G + 128] = up[h4, r1].T
    return wq_h, wk_h, wv_h, woT_h, wcp_h, mkT_h, upT_h


def kernel(L0_attn_x_BSE, L0_attn_w_q, L0_attn_w_k, L0_attn_w_v, L0_attn_w_o,
           L0_ffn_rail_maker, L0_ffn_rail_up_proj, L0_ffn_w_cproj):
    x = np.asarray(L0_attn_x_BSE, dtype=np.float32)
    xt_full = np.ascontiguousarray(x.transpose(0, 2, 1))
    wq_h, wk_h, wv_h, woT_h, wcp_h, mkT_h, upT_h = _host_layouts(
        L0_attn_w_q, L0_attn_w_k, L0_attn_w_v, L0_attn_w_o,
        L0_ffn_rail_maker, L0_ffn_rail_up_proj, L0_ffn_w_cproj)
    nc = _get_nc(debug=False)
    in_maps = []
    for c in range(8):
        in_maps.append({
            "xt": np.ascontiguousarray(xt_full[2 * c:2 * c + 2]),
            "wq": wq_h, "wk": wk_h, "wv": wv_h, "woT": woT_h,
            "wcp": wcp_h, "mkT": mkT_h, "upT": upT_h,
        })
    res = run_bass_kernel_spmd(nc, in_maps, core_ids=list(range(8)))
    shards = [np.asarray(r["y"]) for r in res.results]
    out = np.concatenate(shards, axis=0)
    return np.ascontiguousarray(out.transpose(0, 2, 1)).astype(np.float32)


# version 05
# speedup vs baseline: 1.0504x; 1.0200x over previous
import numpy as np
import ml_dtypes
import concourse.bass as bass
import concourse.mybir as mybir
from concourse.bass_utils import run_bass_kernel_spmd

F32 = mybir.dt.float32
BF16 = mybir.dt.bfloat16
AF = mybir.ActivationFunctionType
ALU = mybir.AluOpType
AX = mybir.AxisListType

S = 2048
E = 1024
NCH = 8
NPAIR = 8
NKT = 16
NSPAN = 4
SUB = 512
NSUB = 4
FH = 4
NG = 32
LN8 = 2.0794415416798357
MSHIFT = 60.0

SB = 16512

O_XBUF = 0
O_LNBUF = 65536
O_AVT = 98304
O_RAILS = 131072
O_STAT = 139264
O_QA0 = 147456
O_QA1 = 151552
O_KA0 = 155648
O_KA1 = 159744
O_VAUG = 163840
O_SQT = 168448
O_EXP0 = 180224
O_EXP1 = 184320
O_AVN = 188416
O_WSL = 192512
O_CONST = 208896

ENGS = ("pe", "act", "dve", "gp", "sy")
NPOOL = 8


class EM:

    def __init__(self):
        self.ops = {e: [] for e in ENGS}
        self.cnt = {e: 0 for e in ENGS}
        self.sem_names = ["pe", "act", "dve"] + \
            ["%s%d" % (q, i) for q in ("gp", "sy") for i in range(NPOOL)]
        self.scnt = {sname: 0 for sname in self.sem_names}

    def op(self, eng, fn, waits=(), inc=False):
        w = {}
        for ev in waits:
            if ev is None:
                continue
            for k, v in ev.items():
                assert v <= self.scnt[k], "wait on future event"
                if w.get(k, -1) < v:
                    w[k] = v
        evt = None
        if inc:
            if eng in ("gp", "sy"):
                i = self.cnt[eng]
                self.cnt[eng] += 1
                sname = "%s%d" % (eng, i % NPOOL)
                if i >= NPOOL:
                    prev = 16 * (i // NPOOL)
                    if w.get(sname, -1) < prev:
                        w[sname] = prev
                self.scnt[sname] += 16
                evt = {sname: self.scnt[sname]}
                self.ops[eng].append((fn, w, sname))
                return evt
            else:
                self.cnt[eng] += 1
                self.scnt[eng] += 1
                evt = {eng: self.scnt[eng]}
                self.ops[eng].append((fn, w, eng))
                return evt
        self.ops[eng].append((fn, w, None))
        return evt


def ev_max(*evs):
    out = {}
    for ev in evs:
        if ev is None:
            continue
        for k, v in ev.items():
            if out.get(k, -1) < v:
                out[k] = v
    return out or None


def build_program(debug=False, trunc=None):
    nc = bass.Bass("TRN2", target_bir_lowering=False)

    xt = nc.dram_tensor("xt", [2, E, S], F32, kind="ExternalInput").ap()
    wq = nc.dram_tensor("wq", [E, E], F32, kind="ExternalInput").ap()
    wk = nc.dram_tensor("wk", [E, E], F32, kind="ExternalInput").ap()
    wv = nc.dram_tensor("wv", [E, E], F32, kind="ExternalInput").ap()
    woT = nc.dram_tensor("woT", [E, E], F32, kind="ExternalInput").ap()
    wcp = nc.dram_tensor("wcp", [8, 128, 128, 128], F32, kind="ExternalInput").ap()
    mkT = nc.dram_tensor("mkT", [128, 1024], F32, kind="ExternalInput").ap()
    upT = nc.dram_tensor("upT", [128, 4096], F32, kind="ExternalInput").ap()
    y = nc.dram_tensor("y", [2, E, S], F32, kind="ExternalOutput").ap()
    wcpb = nc.dram_tensor("wcpb", [8, 128, 128, 128], BF16, kind="Internal").ap()
    xsp = nc.dram_tensor("xsp", [2, E, S], F32, kind="Internal").ap()

    dbg = {}
    if debug:
        for name, shape, dt in [
            ("d_ln1", [128, 16384], BF16), ("d_qa0", [128, 2048], BF16),
            ("d_ka0", [128, 2048], BF16), ("d_qa1", [128, 2048], BF16),
            ("d_ka1", [128, 2048], BF16),
            ("d_vaug", [128, 2080], BF16), ("d_avt", [128, 2048], BF16),
            ("d_x1", [E, S], F32), ("d_exp", [128, 128], BF16),
            ("d_ln2", [128, 4096], BF16), ("d_ht", [128, 4096], BF16),
            ("d_avn", [128, 512], F32), ("d_rc", [128, 2048], F32),
            ("d_avpsum", [128, 512], F32), ("d_exp0", [128, 2048], BF16),
        ]:
            dbg[name] = nc.dram_tensor(name, shape, dt, kind="ExternalOutput").ap()

    ident_np = np.eye(128, dtype=np.float32)
    mask_np = np.zeros((128, 128), dtype=ml_dtypes.bfloat16)
    for k in range(128):
        mask_np[k, k:] = 1.0
    ones128_np = np.ones((128, 1), dtype=ml_dtypes.bfloat16)
    ind0_np = np.zeros((128, 65), dtype=ml_dtypes.bfloat16)
    ind0_np[0:64, 64] = 1.0
    ind1_np = np.zeros((128, 65), dtype=ml_dtypes.bfloat16)
    ind1_np[64:128, 64] = 1.0
    onesf_np = np.ones((1, 128), dtype=np.float32)
    h_identf = nc.inline_tensor(ident_np, name="c_identf")
    h_mask = nc.inline_tensor(np.asarray(mask_np), name="c_mask")
    h_ones = nc.inline_tensor(np.asarray(ones128_np), name="c_ones")
    h_ind0 = nc.inline_tensor(np.asarray(ind0_np), name="c_ind0")
    h_ind1 = nc.inline_tensor(np.asarray(ind1_np), name="c_ind1")
    h_onesf = nc.inline_tensor(onesf_np, name="c_onesf")

    def sb(name, shape, dt, off):
        return nc.alloc_sbuf_tensor_at(name, shape, dt, offset=SB + off)

    xbuf = sb("xbuf", [128, 8 * 2048], F32, O_XBUF)
    lnbuf = sb("lnbuf", [128, 8 * 2048], BF16, O_LNBUF)
    avt = sb("avt", [128, 8 * 2048], BF16, O_AVT)
    hT = sb("hT", [128, 65536], BF16, O_XBUF)
    rails = sb("rails", [128, 4096], BF16, O_RAILS)
    stat = sb("stat", [128, 2048], F32, O_STAT)
    qa = [sb("qa0", [128, 2048], BF16, O_QA0), sb("qa1", [128, 2048], BF16, O_QA1)]
    ka = [sb("ka0", [128, 2048], BF16, O_KA0), sb("ka1", [128, 2048], BF16, O_KA1)]
    vaug = sb("vaug", [128, 2080], BF16, O_VAUG)
    sqt = sb("sqt", [128, 2048], BF16, O_SQT)
    xball = sb("xball", [128, 4096], BF16, O_SQT)
    expS = [sb("expS0", [128, 2048], BF16, O_EXP0), sb("expS1", [128, 2048], BF16, O_EXP1)]
    avn = sb("avn", [128, 512], F32, O_AVN)
    lnsq = sb("lnsq", [128, 1024], BF16, O_AVN + 2048)
    wslq = sb("wslq", [128, 1024], BF16, O_WSL)
    wslk = sb("wslk", [128, 1024], BF16, O_WSL + 2048)
    wslv = sb("wslv", [128, 1024], BF16, O_WSL + 4096)
    wost = [sb("wost0", [128, 1024], BF16, O_WSL + 6144),
            sb("wost1", [128, 1024], BF16, O_WSL + 8192)]
    x1b = [sb("x1A", [128, 4096], F32, O_QA0), sb("x1B", [128, 4096], F32, O_VAUG)]
    wbuf = sb("wbuf", [128, 4096], BF16, O_EXP0)
    outb = [sb("outA", [128, 512], F32, O_AVN), sb("outB", [128, 512], F32, O_AVN + 2048)]
    ln2 = sb("ln2", [128, 4096], BF16, O_WSL)
    xb512 = sb("xb512", [128, 512], BF16, O_WSL + 8192)
    xq512 = sb("xq512", [128, 512], BF16, O_WSL + 9216)
    makbuf = sb("makbuf", [128, 1024], BF16, O_EXP0)
    upbuf = sb("upbuf", [128, 4096], BF16, O_SQT)
    masktri = sb("masktri", [128, 128], BF16, O_CONST)
    ones1 = sb("ones1", [128, 1], BF16, O_CONST + 256)
    ind0 = sb("ind0", [128, 65], BF16, O_CONST + 288)
    ind1 = sb("ind1", [128, 65], BF16, O_CONST + 448)
    onesf = sb("onesf", [1, 128], F32, O_CONST + 608)
    crow = sb("crow", [128, 32], F32, O_CONST + 1120)
    identf = sb("identf", [128, 128], F32, O_CONST + 1248)

    PA = nc.alloc_psum_tensor("PA", [128, 2048], F32)
    PBb = [nc.alloc_psum_tensor("PB%d" % i, [128, 512], F32) for i in range(4)]

    em = EM()

    bank_w = {}
    bank_r = {}

    def bw(b):
        return bank_w.get(b)

    def br(b):
        return bank_r.get(b)

    def set_bw(b, ev):
        bank_w[b] = ev_max(bank_w.get(b), ev)

    def set_br(b, ev):
        bank_r[b] = ev_max(bank_r.get(b), ev)

    class _Stop(Exception):
        pass

    def _ck(name):
        if trunc == name:
            raise _Stop()

    try:
        e_c = []
        for dst, src in [(masktri, h_mask), (ones1, h_ones), (ind0, h_ind0),
                         (ind1, h_ind1), (onesf, h_onesf), (identf, h_identf)]:
            e_c.append(em.op("gp", lambda g, d=dst, s=src: g.dma_start(d[:], s.ap()), inc=True))
        e_consts = ev_max(*e_c)

        e_cr = []
        for col, val in [(0, 1.0 / 1024.0), (1, 1e-5), (2, 0.125), (4, -0.5), (5, 0.5)]:
            e_cr.append(em.op("dve", lambda v, c=col, x=val: v.memset(crow[:, c:c + 1], x), inc=True))
        e_crow = ev_max(*e_cr)


        e_mk = em.op("gp", lambda g: g.dma_start(makbuf[:], mkT), inc=True)
        e_up = em.op("gp", lambda g: g.dma_start(upbuf[:], upT), inc=True)

        prev_evac = {}
        rails_ev = []
        for G in range(NG):
            slot = G % 4
            col = slot * 512
            e_mm = None
            for t in range(4):
                e_mm = em.op("pe", lambda p, t=t, col=col, G=G: p.matmul(
                    PA[32 * t:32 * t + 32, col:col + 128],
                    makbuf[32 * t:32 * t + 32, 32 * G:32 * G + 32],
                    upbuf[32 * t:32 * t + 32, 128 * G:128 * G + 128],
                    start=True, stop=True, tile_position=(32 * t, 32 * t)),
                    waits=[e_mk, e_up, prev_evac.get(slot)], inc=(t == 3))
            e_ev = em.op("act", lambda a, col=col, G=G: a.copy(
                rails[:, G * 128:(G + 1) * 128], PA[:, col:col + 128]),
                waits=[e_mm], inc=True)
            prev_evac[slot] = e_ev
            rails_ev.append(e_ev)
        e_rails = ev_max(*rails_ev)
        e_mcomb_pe = em.cnt["pe"]
        e_mcomb = {"pe": e_mcomb_pe}
        if debug:
            for tz in (qa[0], qa[1], ka[0], ka[1], stat, avn, vaug,
                       expS[0], expS[1]):
                e_crow = ev_max(e_crow, em.op(
                    "dve", lambda v, t=tz: v.memset(t[:], 0.0),
                    waits=[e_mcomb, e_crow], inc=True))
        for b in range(4):
            set_br("A%d" % b, e_rails)

        wcp_pending = list(range(8))
        e_wcp = [None] * 8

        def emit_wcp_cast():
            if wcp_pending:
                ec = wcp_pending.pop(0)
                e_wcp[ec] = em.op("gp", lambda g, ec=ec: g.dma_start(wcpb[ec], wcp[ec]), inc=True)

        e_hT_last_read = None
        e_spill = [None, None]
        e_lnbuf_guard = None
        e_avt_guard = None
        e_x1A_readers = None
        e_x1B_readers = None
        e_exp_guard = [None, None]
        e_avnregion_guard = None
        e_wsl_guard = [None, None, None]
        e_wost_guard = [None, None]
        e_ln2_guard = None
        e_stat_r = None
        e_store_last = None

        for s in range(2):
            e_xld = []
            for c in range(NCH):
                w = [e_hT_last_read, e_spill[s - 1] if s == 1 else None]
                e_xld.append(em.op("sy", lambda g, c=c, s=s: g.dma_start(
                    xbuf[:, c * 2048:(c + 1) * 2048], xt[s, 128 * c:128 * (c + 1), :]),
                    waits=w, inc=True))

            e_ms = []
            e_ms.append(em.op("dve", lambda v: v.memset(ka[0][64:65, :], 1.0),
                              waits=[e_x1A_readers, e_crow], inc=True))
            e_ms.append(em.op("dve", lambda v: v.memset(ka[1][64:65, :], 1.0),
                              waits=[e_x1A_readers], inc=True))
            vv = vaug.ap().rearrange("p (t c) -> p t c", c=130)
            e_ms.append(em.op("dve", lambda v, a=vv: v.memset(a[:, :, 64:65], 1.0),
                              waits=[e_x1B_readers], inc=True))
            e_ms.append(em.op("dve", lambda v, a=vv: v.memset(a[:, :, 129:130], 1.0),
                              waits=[e_x1B_readers], inc=True))
            e_kaug_init = ev_max(*e_ms)

            e_ln_done = {}
            e_xball_r = None
            for sp in range(NSPAN):
                e_sx = None
                e_sq = None
                sqev = {}
                for c in range(NCH):
                    e1 = em.op("dve", lambda v, c=c, sp=sp: v.tensor_copy(
                        xball[:, c * 512:(c + 1) * 512],
                        xbuf[:, c * 2048 + sp * 512: c * 2048 + sp * 512 + 512]),
                        waits=[e_xld[c], e_xball_r, e_mcomb, e_consts, e_crow,
                               e_x1B_readers], inc=True)
                    e2 = em.op("dve", lambda v, c=c: v.tensor_mul(
                        lnsq[:, (c % 2) * 512:(c % 2) * 512 + 512],
                        xball[:, c * 512:(c + 1) * 512],
                        xball[:, c * 512:(c + 1) * 512]),
                        waits=[e1, e_avnregion_guard, sqev.get(c - 2)], inc=True)
                    e_sx = em.op("pe", lambda p, c=c: p.matmul(
                        PBb[0][0:1, 0:512], ones1[:], xball[:, c * 512:(c + 1) * 512],
                        start=(c == 0), stop=(c == NCH - 1)),
                        waits=[e1, br("B0")], inc=True)
                    e_sq = em.op("pe", lambda p, c=c: p.matmul(
                        PBb[1][0:1, 0:512], ones1[:], lnsq[:, (c % 2) * 512:(c % 2) * 512 + 512],
                        start=(c == 0), stop=(c == NCH - 1)),
                        waits=[e2, br("B1")], inc=True)
                    sqev[c] = e_sq
                set_bw("B0", e_sx)
                set_bw("B1", e_sq)
                e_xball_r = ev_max(e_sx, e_sq)
                e_mean = em.op("act", lambda a: a.activation(
                    stat[0:1, 512:1024], PBb[0][0:1, 0:512], AF.Copy, scale=crow[0:1, 0:1]),
                    waits=[e_sx, e_stat_r], inc=True)
                e_m2 = em.op("dve", lambda v: v.tensor_mul(
                    stat[0:1, 1024:1536], stat[0:1, 512:1024], stat[0:1, 512:1024]),
                    waits=[e_mean], inc=True)
                e_var = em.op("dve", lambda v: v.scalar_tensor_tensor(
                    stat[0:1, 1536:2048], PBb[1][0:1, 0:512], 1.0 / 1024.0,
                    stat[0:1, 1024:1536], ALU.mult, ALU.subtract),
                    waits=[e_sq, e_m2], inc=True)
                e_lnv = em.op("act", lambda a: a.activation(
                    stat[0:1, 1024:1536], stat[0:1, 1536:2048], AF.Ln,
                    bias=crow[0:1, 1:2]), waits=[e_var], inc=True)
                e_rstd = em.op("act", lambda a: a.activation(
                    stat[0:1, 1536:2048], stat[0:1, 1024:1536], AF.Exp,
                    scale=crow[0:1, 4:5]), waits=[e_lnv], inc=True)
                set_br("B0", e_mean)
                set_br("B1", e_var)
                e_stat_r = ev_max(e_mean, e_var)
                e_mb = em.op("pe", lambda p: p.matmul(
                    PA[:, 0:512], onesf[:], stat[0:1, 512:1024], start=True, stop=True),
                    waits=[e_mean, br("A0")], inc=True)
                e_rb = em.op("pe", lambda p: p.matmul(
                    PA[:, 512:1024], onesf[:], stat[0:1, 1536:2048], start=True, stop=True),
                    waits=[e_rstd, br("A1")], inc=True)
                set_bw("A0", e_mb)
                set_bw("A1", e_rb)
                last = None
                for c in range(NCH):
                    esub = em.op("dve", lambda v, c=c, sp=sp: v.tensor_tensor(
                        stat[:, 0:512], xbuf[:, c * 2048 + sp * 512: c * 2048 + sp * 512 + 512],
                        PA[:, 0:512], ALU.subtract),
                        waits=[e_mb, e_lnbuf_guard, last], inc=True)
                    last = em.op("dve", lambda v, c=c, sp=sp: v.tensor_tensor(
                        lnbuf[:, c * 2048 + sp * 512: c * 2048 + sp * 512 + 512],
                        stat[:, 0:512], PA[:, 512:1024], ALU.mult),
                        waits=[e_rb, esub], inc=True)
                    e_ln_done[(sp, c)] = last
                set_br("A0", last)
                set_br("A1", last)
            e_ln_all = ev_max(*[e_ln_done[k] for k in e_ln_done])
            e_lnbuf_guard = None

            if debug and s == 0:
                em.op("sy", lambda g: g.dma_start(dbg["d_ln1"], lnbuf[:]),
                      waits=[e_ln_all], inc=True)
            _ck("ln1")

            wq_r = wq.rearrange("(c p) n -> p c n", p=128)
            wk_r = wk.rearrange("(c p) n -> p c n", p=128)
            wv_r = wv.rearrange("(c p) n -> p c n", p=128)
            woT_r = woT.rearrange("(c p) n -> p c n", p=128)

            def load_wsl(p_idx):
                res = []
                for i, (buf, src) in enumerate([(wslq, wq_r), (wslk, wk_r), (wslv, wv_r)]):
                    res.append(em.op("gp", lambda g, b=buf, sr=src, p=p_idx: g.dma_start(
                        b[:], sr[:, :, p * 128:(p + 1) * 128]),
                        waits=[e_wsl_guard[i], e_ln2_guard], inc=True))
                return res

            e_wsl = load_wsl(0)
            emit_wcp_cast()

            sc_toggle = [0]
            pend_tr = []
            e_avt_ev = None
            e_qa_rd = [e_x1A_readers, e_x1A_readers]
            e_ka_rd = [e_x1A_readers, e_x1A_readers]
            e_sqt_rd = e_x1B_readers
            e_avn_rd = e_avnregion_guard

            for p_idx in range(NPAIR):
                e_kaug_ev = [{}, {}]
                e_ksq = [None] * NSPAN
                for sp in range(NSPAN):
                    bk = "A%d" % (sp % 2)
                    col = (sp % 2) * 512
                    e_mm = None
                    for c in range(NCH):
                        e_mm = em.op("pe", lambda p, c=c, sp=sp, col=col: p.matmul(
                            PA[:, col:col + 512], wslk[:, c * 128:(c + 1) * 128],
                            lnbuf[:, c * 2048 + sp * 512: c * 2048 + sp * 512 + 512],
                            start=(c == 0), stop=(c == NCH - 1)),
                            waits=[e_wsl[1], e_ln_done[(sp, c)], br(bk), e_kaug_init],
                            inc=(c == NCH - 1))
                    set_bw(bk, e_mm)
                    e_wsl_guard[1] = ev_max(e_wsl_guard[1], e_mm)
                    ev0 = em.op("act", lambda a, sp=sp, col=col: a.activation(
                        ka[0][0:64, sp * 512:(sp + 1) * 512], PA[0:64, col:col + 512],
                        AF.Copy, scale=crow[0:64, 2:3]),
                        waits=[e_mm, e_ka_rd[0]], inc=True)
                    ev1 = em.op("act", lambda a, sp=sp, col=col: a.activation(
                        expS[1][64:128, sp * 512:(sp + 1) * 512], PA[64:128, col:col + 512],
                        AF.Copy, scale=crow[64:128, 2:3]),
                        waits=[e_mm, e_exp_guard[1]], inc=True)
                    e_kaug_ev[0][sp] = ev0
                    e_kaug_ev[1][sp] = ev1
                    esq = em.op("act", lambda a, sp=sp, col=col: a.activation(
                        sqt[:, sp * 512:(sp + 1) * 512], PA[:, col:col + 512], AF.Square),
                        waits=[e_mm, e_sqt_rd], inc=True)
                    e_ksq[sp] = esq
                    set_br(bk, ev_max(ev0, ev1, esq))
                e_sqt_rd = None
                if p_idx == 0:
                    _ck("kproj")
                e_shiftK = em.op("gp", lambda g: g.dma_start(
                    ka[1][0:64, :], expS[1][64:128, :]),
                    waits=[ev_max(*[e_kaug_ev[1][x] for x in range(NSPAN)]), e_ka_rd[1]],
                    inc=True)
                e_kmax = [None, None]
                for sp in range(NSPAN):
                    e_st0 = em.op("pe", lambda p, sp=sp: p.matmul(
                        PA[0:65, 1024:1536], ind0[:], sqt[:, sp * 512:(sp + 1) * 512],
                        start=True, stop=True), waits=[e_ksq[sp], br("A2")], inc=True)
                    set_bw("A2", e_st0)
                    e_st1 = em.op("pe", lambda p, sp=sp: p.matmul(
                        PA[0:65, 1536:2048], ind1[:], sqt[:, sp * 512:(sp + 1) * 512],
                        start=True, stop=True), waits=[e_ksq[sp], br("A3")], inc=True)
                    set_bw("A3", e_st1)
                    em0 = em.op("dve", lambda v, sp=sp: v.reduce_max(
                        stat[64:65, 1030 + sp:1031 + sp], PA[64:65, 1024:1536], axis=AX.X),
                        waits=[e_st0], inc=True)
                    em1 = em.op("dve", lambda v, sp=sp: v.reduce_max(
                        stat[64:65, 1050 + sp:1051 + sp], PA[64:65, 1536:2048], axis=AX.X),
                        waits=[e_st1], inc=True)
                    e_kmax[0] = ev_max(e_kmax[0], em0)
                    e_kmax[1] = ev_max(e_kmax[1], em1)
                    set_br("A2", em0)
                    set_br("A3", em1)
                e_mx0 = em.op("dve", lambda v: v.reduce_max(
                    stat[64:65, 1040:1041], stat[64:65, 1030:1034], axis=AX.X),
                    waits=[e_kmax[0]], inc=True)
                e_mx1 = em.op("dve", lambda v: v.reduce_max(
                    stat[64:65, 1055:1056], stat[64:65, 1050:1054], axis=AX.X),
                    waits=[e_kmax[1]], inc=True)
                e_l0 = em.op("act", lambda a: a.activation(
                    stat[64:65, 1042:1043], stat[64:65, 1040:1041], AF.Ln),
                    waits=[e_mx0], inc=True)
                e_l1 = em.op("act", lambda a: a.activation(
                    stat[64:65, 1057:1058], stat[64:65, 1055:1056], AF.Ln),
                    waits=[e_mx1], inc=True)
                e_lnc0 = em.op("dve", lambda v: v.tensor_scalar(
                    stat[64:65, 1044:1045], stat[64:65, 1042:1043], 0.5, -LN8,
                    ALU.mult, ALU.add), waits=[e_l0], inc=True)
                e_lnc1 = em.op("dve", lambda v: v.tensor_scalar(
                    stat[64:65, 1059:1060], stat[64:65, 1057:1058], 0.5, -LN8,
                    ALU.mult, ALU.add), waits=[e_l1], inc=True)

                e_qaug_ev = [{}, {}]
                e_mrow = [{}, {}]
                for sp in range(NSPAN):
                    bk = "A%d" % (sp % 2)
                    col = (sp % 2) * 512
                    e_mm = None
                    for c in range(NCH):
                        e_mm = em.op("pe", lambda p, c=c, sp=sp, col=col: p.matmul(
                            PA[:, col:col + 512], wslq[:, c * 128:(c + 1) * 128],
                            lnbuf[:, c * 2048 + sp * 512: c * 2048 + sp * 512 + 512],
                            start=(c == 0), stop=(c == NCH - 1)),
                            waits=[e_wsl[0], br(bk)], inc=(c == NCH - 1))
                    set_bw(bk, e_mm)
                    e_wsl_guard[0] = ev_max(e_wsl_guard[0], e_mm)
                    ev0 = em.op("act", lambda a, sp=sp, col=col: a.copy(
                        qa[0][0:64, sp * 512:(sp + 1) * 512], PA[0:64, col:col + 512]),
                        waits=[e_mm, e_qa_rd[0]], inc=True)
                    ev1 = em.op("act", lambda a, sp=sp, col=col: a.copy(
                        expS[0][64:128, sp * 512:(sp + 1) * 512], PA[64:128, col:col + 512]),
                        waits=[e_mm, e_exp_guard[0]], inc=True)
                    e_qaug_ev[0][sp] = ev0
                    e_qaug_ev[1][sp] = ev1
                    esq = em.op("act", lambda a, sp=sp, col=col: a.activation(
                        sqt[:, sp * 512:(sp + 1) * 512], PA[:, col:col + 512], AF.Square),
                        waits=[e_mm], inc=True)
                    set_br(bk, ev_max(ev0, ev1, esq))
                    e_st0 = em.op("pe", lambda p, sp=sp: p.matmul(
                        PA[0:65, 1024:1536], ind0[:], sqt[:, sp * 512:(sp + 1) * 512],
                        start=True, stop=True), waits=[esq, br("A2")], inc=True)
                    set_bw("A2", e_st0)
                    e_st1 = em.op("pe", lambda p, sp=sp: p.matmul(
                        PA[0:65, 1536:2048], ind1[:], sqt[:, sp * 512:(sp + 1) * 512],
                        start=True, stop=True), waits=[esq, br("A3")], inc=True)
                    set_bw("A3", e_st1)
                    a0 = em.op("act", lambda a: a.activation(
                        stat[64:65, 0:512], PA[64:65, 1024:1536], AF.Ln),
                        waits=[e_st0, e_mrow[1].get(sp - 1)], inc=True)
                    set_br("A2", a0)
                    b0 = em.op("act", lambda a: a.activation(
                        stat[64:65, 512:1024], stat[64:65, 0:512], AF.Exp,
                        scale=crow[64:65, 5:6], bias=stat[64:65, 1044:1045]),
                        waits=[a0, e_lnc0], inc=True)
                    m0 = em.op("dve", lambda v, sp=sp: v.tensor_scalar(
                        qa[0][64:65, sp * 512:(sp + 1) * 512], stat[64:65, 512:1024],
                        MSHIFT, -1.0, ALU.subtract, ALU.mult),
                        waits=[b0, e_qa_rd[0]], inc=True)
                    a1 = em.op("act", lambda a: a.activation(
                        stat[64:65, 0:512], PA[64:65, 1536:2048], AF.Ln),
                        waits=[e_st1, m0], inc=True)
                    set_br("A3", a1)
                    b1 = em.op("act", lambda a: a.activation(
                        stat[64:65, 512:1024], stat[64:65, 0:512], AF.Exp,
                        scale=crow[64:65, 5:6], bias=stat[64:65, 1059:1060]),
                        waits=[a1, e_lnc1, m0], inc=True)
                    m1 = em.op("dve", lambda v, sp=sp: v.tensor_scalar(
                        qa[1][64:65, sp * 512:(sp + 1) * 512], stat[64:65, 512:1024],
                        MSHIFT, -1.0, ALU.subtract, ALU.mult),
                        waits=[b1, e_qa_rd[1]], inc=True)
                    e_mrow[0][sp] = m0
                    e_mrow[1][sp] = m1
                if p_idx == 0:
                    _ck("qproj")
                e_shiftQ = em.op("gp", lambda g: g.dma_start(
                    qa[1][0:64, :], expS[0][64:128, :]),
                    waits=[ev_max(*[e_qaug_ev[1][x] for x in range(NSPAN)]), e_qa_rd[1]],
                    inc=True)
                e_qaug_ev[1] = {sp: e_shiftQ for sp in range(NSPAN)}
                e_kaug_ev[1] = {x: e_shiftK for x in range(NSPAN)}
                e_exp_guard = [ev_max(e_exp_guard[0], e_shiftQ),
                               ev_max(e_exp_guard[1], e_shiftK)]
                e_qa_rd = [None, None]
                e_ka_rd = [None, None]

                e_vev = [None] * NKT
                for tt in range(NKT):
                    bk = "A%d" % (tt % 2)
                    col = (tt % 2) * 512
                    e_mm = None
                    for c in range(NCH):
                        e_mm = em.op("pe", lambda p, c=c, tt=tt, col=col: p.matmul(
                            PA[:, col:col + 128],
                            lnbuf[:, c * 2048 + tt * 128: c * 2048 + tt * 128 + 128],
                            wslv[:, c * 128:(c + 1) * 128],
                            start=(c == 0), stop=(c == NCH - 1)),
                            waits=[e_wsl[2], br(bk)], inc=(c == NCH - 1))
                    set_bw(bk, e_mm)
                    e_wsl_guard[2] = ev_max(e_wsl_guard[2], e_mm)
                    d0 = em.op("dve", lambda v, tt=tt, col=col: v.tensor_copy(
                        vaug[:, tt * 130:tt * 130 + 64], PA[:, col:col + 64]),
                        waits=[e_mm, e_kaug_init], inc=True)
                    d1 = em.op("dve", lambda v, tt=tt, col=col: v.tensor_copy(
                        vaug[:, tt * 130 + 65:tt * 130 + 129], PA[:, col + 64:col + 128]),
                        waits=[d0], inc=True)
                    e_vev[tt] = d1
                    set_br(bk, d1)

                if p_idx == 0:
                    _ck("vproj")
                if p_idx + 1 < NPAIR:
                    e_wsl = load_wsl(p_idx + 1)
                emit_wcp_cast()

                if debug and s == 0 and p_idx == 0:
                    em.op("sy", lambda g: g.dma_start(dbg["d_qa0"], qa[0][:]),
                          waits=[ev_max(*[e_qaug_ev[0][x] for x in range(4)]),
                                 ev_max(*[e_mrow[0][x] for x in range(4)])], inc=True)
                    em.op("sy", lambda g: g.dma_start(dbg["d_qa1"], qa[1][:]),
                          waits=[ev_max(*[e_qaug_ev[1][x] for x in range(4)]),
                                 ev_max(*[e_mrow[1][x] for x in range(4)])], inc=True)
                    em.op("sy", lambda g: g.dma_start(dbg["d_ka0"], ka[0][:]),
                          waits=[ev_max(*[e_kaug_ev[0][x] for x in range(4)]), e_kaug_init], inc=True)
                    em.op("sy", lambda g: g.dma_start(dbg["d_ka1"], ka[1][:]),
                          waits=[ev_max(*[e_kaug_ev[1][x] for x in range(4)]), e_kaug_init], inc=True)
                    em.op("sy", lambda g: g.dma_start(dbg["d_vaug"], vaug[:]),
                          waits=[ev_max(*e_vev)], inc=True)

                def drain_tr(item):
                    nonlocal e_avn_rd, e_avt_ev, e_avt_guard
                    b = item["b"]
                    tb = b - 1 if b > 0 else 0
                    thh = item["hh"]
                    tp = item["p"]
                    prow = thh * 64
                    e_tr = None
                    for qt in range(4 * b, 4 * b + 4):
                        e_tr = em.op("pe", lambda p, qt=qt, thh=thh, tb=tb: p.transpose(
                            PBb[tb][0:64 + thh * 64, (qt % 4) * 128:(qt % 4) * 128 + 128],
                            avn[:, (qt % 4) * 128:(qt % 4) * 128 + 64 + thh * 64],
                            identf[:]),
                            waits=[item["nrm"][qt], br("B%d" % tb)], inc=(qt % 4 == 3))
                    set_bw("B%d" % tb, e_tr)
                    e_ae = em.op("act", lambda a, tb=tb, b=b, tp=tp, prow=prow: a.copy(
                        avt[prow:prow + 64,
                            tp * 2048 + b * 512: tp * 2048 + b * 512 + 512],
                        PBb[tb][prow:prow + 64, 0:512]),
                        waits=[e_tr, e_avt_guard], inc=True)
                    set_br("B%d" % tb, e_ae)
                    e_avn_rd = ev_max(e_avn_rd, e_tr)
                    e_avt_ev = ev_max(e_avt_ev, e_ae)

                for hh in range(2):
                    q_t = qa[hh]
                    k_t = ka[hh]
                    rows = (0, 65)
                    e_exp_sp = {}
                    e_mask_kt = {}
                    e_bankdone = {}
                    e_nrm = {}
                    e_avlast = {}
                    e_dbg_stat = None
                    def emit_scores(kt):
                        q0 = kt * 128
                        nsp = -(-(S - q0) // 512)
                        for j in range(nsp):
                            c0 = q0 + j * 512
                            n = min(512, S - c0)
                            bsl = 2 + sc_toggle[0]
                            sc_toggle[0] ^= 1
                            bk = "A%d" % bsl
                            col = bsl * 512
                            spans = list(range(c0 // 512, (c0 + n - 1) // 512 + 1))
                            wts = [br(bk), e_kaug_ev[hh].get(kt // 4),
                                   ev_max(*[e_qaug_ev[hh][x] for x in spans]),
                                   ev_max(*[e_mrow[hh][x] for x in spans])]
                            e_mm = em.op("pe", lambda p, kt=kt, c0=c0, n=n, col=col,
                                         q_t=q_t, k_t=k_t, r=rows: p.matmul(
                                PA[:, col:col + n],
                                k_t[r[0]:r[1], kt * 128:kt * 128 + 128],
                                q_t[r[0]:r[1], c0:c0 + n],
                                start=True, stop=True),
                                waits=wts, inc=True)
                            set_bw(bk, e_mm)
                            e_ex = em.op("act", lambda a, kt=kt, c0=c0, n=n, col=col: a.activation(
                                expS[kt % 2][:, c0 - kt * 128:c0 - kt * 128 + n],
                                PA[:, col:col + n], AF.Exp),
                                waits=[e_mm, e_avlast.get(kt - 2),
                                       e_exp_guard[kt % 2]], inc=True)
                            e_exp_sp[(kt, j)] = e_ex
                            set_br(bk, e_ex)
                            if j == 0:
                                e_mask_kt[kt] = em.op("dve", lambda v, kt=kt: v.tensor_tensor(
                                    expS[kt % 2][:, 0:128], expS[kt % 2][:, 0:128],
                                    masktri[:], ALU.mult), waits=[e_ex], inc=True)

                    emit_scores(0)
                    for kt in range(NKT):
                        if kt + 1 < NKT:
                            emit_scores(kt + 1)
                        while pend_tr and pend_tr[0]["hseq"] < 2 * p_idx + hh:
                            drain_tr(pend_tr.pop(0))
                        last_ev = None
                        qts = list(range(kt, NKT)) if kt == 0 else \
                            list(range(kt + 1, NKT)) + [kt]
                        for qt in qts:
                            j = ((qt - kt) * 128) // 512
                            wts = [e_exp_sp[(kt, j)], e_vev[kt]]
                            if qt == kt:
                                wts.append(e_mask_kt[kt])
                            first = (kt == 0 and qt % 4 == 0)
                            if first:
                                wts.append(br("B%d" % (qt // 4)))
                            e_av = em.op("pe", lambda p, kt=kt, qt=qt, hh=hh: p.matmul(
                                PBb[qt // 4][:, (qt % 4) * 128:(qt % 4) * 128 + 65],
                                expS[kt % 2][:, (qt - kt) * 128:(qt - kt) * 128 + 128],
                                vaug[:, kt * 130 + hh * 65: kt * 130 + hh * 65 + 65],
                                start=(kt == 0 and qt % 4 == 0),
                                stop=(kt == qt and qt % 4 == 3)),
                                waits=wts, inc=True)
                            last_ev = e_av
                            if first:
                                set_bw("B%d" % (qt // 4), e_av)
                            if qt == kt:
                                e_bankdone[qt // 4] = ev_max(e_bankdone.get(qt // 4), e_av)
                        if debug and s == 0 and p_idx == 0 and hh == 0 and kt == 0:
                            edump = em.op("sy", lambda g: g.dma_start(
                                dbg["d_exp0"], expS[0][:]),
                                waits=[e_mask_kt[0], last_ev], inc=True)
                            last_ev = ev_max(last_ev, edump)
                        e_avlast[kt] = last_ev
                        e_exp_guard[kt % 2] = last_ev
                        if kt % 4 == 3:
                            b = kt // 4
                            if debug and s == 0 and p_idx == 0 and hh == 0 and b == 0:
                                e_cpy = None
                                for qt_ in range(4):
                                    e_cpy = em.op("dve", lambda v, qt_=qt_: v.tensor_copy(
                                        outb[1][:, qt_ * 128:qt_ * 128 + 65],
                                        PBb[0][:, qt_ * 128:qt_ * 128 + 65]),
                                        waits=[e_bankdone[0], e_cpy], inc=True)
                                for qt_ in range(4):
                                    em.op("sy", lambda g, qt_=qt_: g.dma_start(
                                        dbg["d_avpsum"][:, qt_ * 128:qt_ * 128 + 65],
                                        outb[1][:, qt_ * 128:qt_ * 128 + 65]),
                                        waits=[e_cpy], inc=True)
                            for qt in range(4 * b, 4 * b + 4):
                                rc = em.op("dve", lambda v, qt=qt: v.reciprocal(
                                    stat[:, 1536 + (qt % 4):1537 + (qt % 4)],
                                    PBb[qt // 4][:, (qt % 4) * 128 + 64:(qt % 4) * 128 + 65]),
                                    waits=[e_bankdone[b], e_dbg_stat], inc=True)
                                nr = em.op("dve", lambda v, qt=qt, hh=hh: v.tensor_single_scalar(
                                    avn[:, (qt % 4) * 128 + hh * 64:(qt % 4) * 128 + hh * 64 + 64],
                                    PBb[qt // 4][:, (qt % 4) * 128:(qt % 4) * 128 + 64],
                                    stat[:, 1536 + (qt % 4):1537 + (qt % 4)], ALU.mult),
                                    waits=[rc, e_avn_rd], inc=True)
                                e_nrm[qt] = nr
                            pend_tr.append({"b": b, "hh": hh, "p": p_idx, "hseq": 2 * p_idx + hh, "nrm": {qt: e_nrm[qt] for qt in range(4 * b, 4 * b + 4)}})
                            set_br("B%d" % b, ev_max(*[e_nrm[x] for x in range(4 * b, 4 * b + 4)]))
                            if debug and s == 0 and p_idx == 0 and hh == 0 and b == 0:
                                edmp = em.op("sy", lambda g: g.dma_start(
                                    dbg["d_avn"], avn[:]),
                                    waits=[e_nrm[3]], inc=True)
                                e_avn_rd = ev_max(e_avn_rd, edmp)
                                e_dbg_stat = em.op("sy", lambda g: g.dma_start(
                                    dbg["d_rc"], stat[:]), waits=[e_nrm[3]], inc=True)
                        if p_idx == 0 and hh == 0 and trunc == ("kt%d" % kt):
                            _ck("kt%d" % kt)
                        while pend_tr and (pend_tr[0]["hseq"] == 2 * p_idx + hh
                                           and kt >= 4 * pend_tr[0]["b"] + 5):
                            drain_tr(pend_tr.pop(0))
                            if p_idx == 0 and hh == 0 and trunc == ("tr%d" % kt):
                                _ck("tr%d" % kt)
                        if p_idx == 0 and hh == 0 and trunc == ("tr%d" % kt):
                            _ck("tr%d" % kt)
                    e_qa_rd[hh] = ev_max(e_qa_rd[hh], *[e for e in e_avlast.values() if e])
                    if p_idx == 0 and hh == 0:
                        _ck("head0")
                    if p_idx == 0 and hh == 1:
                        _ck("head1")
                    if p_idx == 1 and hh == 1:
                        _ck("pair1")
                    e_ka_rd[hh] = e_qa_rd[hh]
                    if debug and s == 0 and p_idx == 0 and hh == 0:
                        _ed = em.op("sy", lambda g: g.dma_start(dbg["d_exp"], expS[1][:, 0:128]),
                                    waits=[e_exp_sp[(15, 0)], e_mask_kt[15]], inc=True)
                        e_exp_guard[1] = ev_max(e_exp_guard[1], _ed)
            _ck("pairs")
            e_avt_guard = None
            e_avnregion_guard = None

            while pend_tr:
                drain_tr(pend_tr.pop(0))

            e_wo = [None, None]
            e_xupd = []
            for ec in range(NCH):
                e_wo[ec % 2] = em.op("gp", lambda g, ec=ec: g.dma_start(
                    wost[ec % 2][:], woT_r[:, :, ec * 128:(ec + 1) * 128]),
                    waits=[e_wost_guard[ec % 2]], inc=True)
                for sp in range(NSPAN):
                    bk = "A%d" % (sp % 2)
                    col = (sp % 2) * 512
                    e_mm = None
                    for hc in range(NCH):
                        e_mm = em.op("pe", lambda p, ec=ec, sp=sp, hc=hc, col=col: p.matmul(
                            PA[:, col:col + 512], wost[ec % 2][:, hc * 128:(hc + 1) * 128],
                            avt[:, hc * 2048 + sp * 512: hc * 2048 + sp * 512 + 512],
                            start=(hc == 0), stop=(hc == NCH - 1)),
                            waits=[e_wo[ec % 2], e_avt_ev, br(bk)], inc=(hc == NCH - 1))
                    set_bw(bk, e_mm)
                    e_wost_guard[ec % 2] = ev_max(e_wost_guard[ec % 2], e_mm)
                    e_up2 = em.op("dve", lambda v, ec=ec, sp=sp, col=col: v.tensor_add(
                        xbuf[:, ec * 2048 + sp * 512: ec * 2048 + sp * 512 + 512],
                        xbuf[:, ec * 2048 + sp * 512: ec * 2048 + sp * 512 + 512],
                        PA[:, col:col + 512]), waits=[e_mm], inc=True)
                    set_br(bk, e_up2)
                    e_xupd.append(e_up2)
            e_xupd_all = ev_max(*e_xupd)

            e_sp_list = []
            for c in range(NCH):
                e_sp_list.append(em.op("sy", lambda g, c=c, s=s: g.dma_start(
                    xsp[s, 128 * c:128 * (c + 1), :], xbuf[:, c * 2048:(c + 1) * 2048]),
                    waits=[e_xupd_all], inc=True))
            e_spill[s] = ev_max(*e_sp_list)

            e_dbg_spill = None
            if debug and s == 0:
                _d1 = em.op("sy", lambda g: g.dma_start(dbg["d_x1"], xsp[0]),
                            waits=[e_spill[s]], inc=True)
                _d2 = em.op("sy", lambda g: g.dma_start(dbg["d_avt"], avt[:, 0:2048]),
                            waits=[e_avt_ev], inc=True)
                e_dbg_spill = ev_max(_d1, _d2)
            _ck("oproj")

            e_cp_last = None
            e_x1ld = [None, None]
            e_x1_rd = [None, None]
            e_ln2_rd = None
            wbuf_rd = [None, None]
            e_outb_rd = [None, None]
            xsp_r = xsp[s].rearrange("(c p) t -> p c t", p=128)
            x1v = [x1b[0].ap().rearrange("p (c t) -> p c t", t=512),
                   x1b[1].ap().rearrange("p (c t) -> p c t", t=512)]
            e_gelu_all = None

            for st in range(NSUB):
                xb = st % 2
                e_x1ld[xb] = em.op("sy", lambda g, st=st, xb=xb, xsp_r=xsp_r: g.dma_start(
                    x1v[xb], xsp_r[:, :, st * 512:(st + 1) * 512]),
                    waits=[e_spill[s], e_x1_rd[xb]], inc=True)
                e_sx = None
                e_sq = None
                e_xb5 = None
                for c in range(NCH):
                    w5 = [e_x1ld[xb], e_sx, e_sq]
                    if st == 0 and c == 0:
                        w5 += [e_wost_guard[1], e_ln2_guard]
                    e_xb5 = em.op("dve", lambda v, c=c, xb=xb: v.tensor_copy(
                        xb512[:, 0:512], x1b[xb][:, c * 512:(c + 1) * 512]),
                        waits=w5, inc=True)
                    e_xq5 = em.op("dve", lambda v: v.tensor_mul(
                        xq512[:, 0:512], xb512[:, 0:512], xb512[:, 0:512]),
                        waits=[e_xb5], inc=True)
                    e_sx = em.op("pe", lambda p, c=c: p.matmul(
                        PBb[0][0:1, 0:512], ones1[:], xb512[:, 0:512],
                        start=(c == 0), stop=(c == NCH - 1)),
                        waits=[e_xb5, br("B0")], inc=True)
                    e_sq = em.op("pe", lambda p, c=c: p.matmul(
                        PBb[1][0:1, 0:512], ones1[:], xq512[:, 0:512],
                        start=(c == 0), stop=(c == NCH - 1)),
                        waits=[e_xq5, br("B1")], inc=True)
                set_bw("B0", e_sx)
                set_bw("B1", e_sq)
                e_mean = em.op("act", lambda a: a.activation(
                    stat[0:1, 512:1024], PBb[0][0:1, 0:512], AF.Copy, scale=crow[0:1, 0:1]),
                    waits=[e_sx, e_stat_r], inc=True)
                e_m2 = em.op("dve", lambda v: v.tensor_mul(
                    stat[0:1, 1024:1536], stat[0:1, 512:1024], stat[0:1, 512:1024]),
                    waits=[e_mean], inc=True)
                e_var = em.op("dve", lambda v: v.scalar_tensor_tensor(
                    stat[0:1, 1536:2048], PBb[1][0:1, 0:512], 1.0 / 1024.0,
                    stat[0:1, 1024:1536], ALU.mult, ALU.subtract),
                    waits=[e_sq, e_m2], inc=True)
                e_lnv = em.op("act", lambda a: a.activation(
                    stat[0:1, 1024:1536], stat[0:1, 1536:2048], AF.Ln,
                    bias=crow[0:1, 1:2]), waits=[e_var], inc=True)
                e_rstd = em.op("act", lambda a: a.activation(
                    stat[0:1, 1536:2048], stat[0:1, 1024:1536], AF.Exp,
                    scale=crow[0:1, 4:5]), waits=[e_lnv], inc=True)
                set_br("B0", e_mean)
                set_br("B1", e_var)
                e_stat_r = ev_max(e_mean, e_var)
                e_mb = em.op("pe", lambda p: p.matmul(
                    PA[:, 0:512], onesf[:], stat[0:1, 512:1024], start=True, stop=True),
                    waits=[e_mean, br("A0")], inc=True)
                e_rb = em.op("pe", lambda p: p.matmul(
                    PA[:, 512:1024], onesf[:], stat[0:1, 1536:2048], start=True, stop=True),
                    waits=[e_rstd, br("A1")], inc=True)
                set_bw("A0", e_mb)
                set_bw("A1", e_rb)
                e_ln2ev = []
                _prev2 = None
                for c in range(NCH):
                    w1 = [e_mb, e_ln2_rd, _prev2]
                    if st == 0 and c == 0:
                        w1 += [e_wsl_guard[0], e_wsl_guard[1], e_wsl_guard[2],
                               e_wost_guard[0], e_ln2_guard]
                    esub2 = em.op("dve", lambda v, c=c, xb=xb: v.tensor_tensor(
                        stat[:, 0:512], x1b[xb][:, c * 512:(c + 1) * 512], PA[:, 0:512],
                        ALU.subtract), waits=w1, inc=True)
                    _prev2 = em.op("dve", lambda v, c=c: v.tensor_tensor(
                        ln2[:, c * 512:(c + 1) * 512], stat[:, 0:512], PA[:, 512:1024],
                        ALU.mult), waits=[e_rb, esub2], inc=True)
                    e_ln2ev.append(_prev2)
                e_ln2_all = ev_max(*e_ln2ev)
                set_br("A0", e_ln2_all)
                set_br("A1", e_ln2_all)
                e_x1_rd[xb] = ev_max(e_x1_rd[xb], e_ln2_all)

                if debug and s == 0 and st == 0:
                    em.op("sy", lambda g: g.dma_start(dbg["d_ln2"], ln2[:]),
                          waits=[e_ln2_all], inc=True)

                e_gelu = []
                for h4 in range(FH):
                    for g8 in range(NCH):
                        G = h4 * 8 + g8
                        use_b = (G % 2 == 1)
                        pref = "B" if use_b else "A"
                        e_mm = None
                        for t in range(4):
                            out_ap_fn = (lambda t: (lambda: PBb[t][:, 0:512])) if use_b \
                                else (lambda t: (lambda: PA[:, t * 512:(t + 1) * 512]))
                            e_mm = em.op("pe", lambda p, G=G, t=t, g8=g8, oo=out_ap_fn(t): p.matmul(
                                oo(),
                                rails[32 * t:32 * t + 32, G * 128:G * 128 + 128],
                                ln2[32 * t:32 * t + 32, g8 * 512:(g8 + 1) * 512],
                                start=True, stop=True, tile_position=(32 * t, 0)),
                                waits=[e_rails, e_ln2ev[g8], br("%s%d" % (pref, t))],
                                inc=(t == 3))
                        for t in range(4):
                            set_bw("%s%d" % (pref, t), e_mm)
                        e_ln2_rd = ev_max(e_ln2_rd, e_mm)
                        fc0 = h4 * 32 + 4 * g8
                        wg = [e_mm, e_cp_last]
                        if st == 0 and G == 0:
                            wg += [e_hT_last_read, e_spill[s], e_dbg_spill]
                        if use_b:
                            e_g = None
                            for t in range(4):
                                e_g1 = em.op("act", lambda a, fc0=fc0, t=t: a.activation(
                                    hT[:, (fc0 + t) * 512:(fc0 + t + 1) * 512],
                                    PBb[t][:, 0:512], AF.Gelu),
                                    waits=wg, inc=True)
                                set_br("B%d" % t, e_g1)
                                e_g = ev_max(e_g, e_g1)
                        else:
                            e_g = em.op("act", lambda a, fc0=fc0: a.activation(
                                hT[:, fc0 * 512:(fc0 + 4) * 512], PA[:, 0:2048], AF.Gelu),
                                waits=wg, inc=True)
                            for t in range(4):
                                set_br("A%d" % t, e_g)
                        e_gelu.append(e_g)
                e_gelu_all = ev_max(*e_gelu)

                if debug and s == 0 and st == 0:
                    em.op("sy", lambda g: g.dma_start(dbg["d_ht"], hT[:, 0:4096]),
                          waits=[e_gelu_all], inc=True)

                for ec in range(NCH):
                    cb = "B%d" % (ec % 2)
                    ccol = (ec % 2) * 512
                    e_mm = None
                    for fg in range(8):
                        half = fg % 2
                        wdma_w = [e_wcp[ec], wbuf_rd[half]]
                        if st == 0 and ec == 0 and fg < 2:
                            wdma_w += [e_exp_guard[0], e_exp_guard[1]]
                        e_wd = em.op("sy", lambda g, ec=ec, fg=fg, half=half: g.dma_start(
                            wbuf[:, half * 2048:(half + 1) * 2048],
                            wcpb[ec, :, fg * 16:(fg + 1) * 16, :].rearrange("p a b -> p (a b)")),
                            waits=wdma_w, inc=True)
                        for fc in range(fg * 16, fg * 16 + 16):
                            e_mm = em.op("pe", lambda p, fc=fc, ec=ec, half=half: p.matmul(
                                PBb[ec % 2][:, 0:512],
                                wbuf[:, half * 2048 + (fc % 16) * 128: half * 2048 + (fc % 16) * 128 + 128],
                                hT[:, fc * 512:(fc + 1) * 512],
                                start=(fc == 0), stop=(fc == 127)),
                                waits=[e_gelu_all, e_wd, br(cb) if fc == 0 else None],
                                inc=(fc % 16 == 15))
                        wbuf_rd[half] = ev_max(wbuf_rd[half], e_mm)
                    set_bw(cb, e_mm)
                    e_hT_last_read = ev_max(e_hT_last_read, e_mm)
                    e_cp_last = e_mm
                    e_ev2 = em.op("dve", lambda v, ec=ec, xb=xb: v.tensor_add(
                        outb[ec % 2][:], PBb[ec % 2][:, 0:512],
                        x1b[xb][:, ec * 512:(ec + 1) * 512]),
                        waits=[e_mm, e_x1ld[xb], e_outb_rd[ec % 2]], inc=True)
                    set_br(cb, e_ev2)
                    e_x1_rd[xb] = ev_max(e_x1_rd[xb], e_ev2)
                    e_st2 = em.op("sy", lambda g, ec=ec, st=st, s=s: g.dma_start(
                        y[s, ec * 128:(ec + 1) * 128, st * 512:(st + 1) * 512],
                        outb[ec % 2][:]), waits=[e_ev2], inc=True)
                    e_outb_rd[ec % 2] = e_st2
                    e_store_last = e_st2

            e_x1A_readers = e_x1_rd[0]
            e_x1B_readers = e_x1_rd[1]
            e_exp_guard = [ev_max(wbuf_rd[0], wbuf_rd[1])] * 2
            e_avnregion_guard = ev_max(e_outb_rd[0], e_outb_rd[1])
            e_lnbuf_guard = e_hT_last_read
            e_avt_guard = e_hT_last_read
            e_ln2_guard = e_ln2_rd
            e_wsl_guard = [e_ln2_rd, e_ln2_rd, e_ln2_rd]
            e_wost_guard = [e_ln2_rd,
                            ev_max(e_sx, e_sq, e_x1_rd[0], e_x1_rd[1])]

        pass
    except _Stop:
        pass
    em.op("sy", lambda g: None,
          waits=[{("sy%d" % i): em.scnt["sy%d" % i]} for i in range(NPOOL)
                 if em.scnt["sy%d" % i] > 0])

    from contextlib import ExitStack
    stack = ExitStack()
    sems = {sname: stack.enter_context(nc.semaphore("sem_" + sname))
            for sname in em.sem_names}
    block = stack.enter_context(nc.Block())

    def realize(eng_name):
        def section(eng):
            water = {}
            for fn, waits, inc_sem in em.ops[eng_name]:
                for tgt in sorted(waits):
                    val = waits[tgt]
                    if water.get(tgt, 0) >= val:
                        continue
                    eng.wait_ge(sems[tgt], val)
                    water[tgt] = val
                ins = fn(eng)
                if inc_sem is not None:
                    if ins is None:
                        raise RuntimeError("inc op returned no instruction")
                    ins.then_inc(sems[inc_sem], 16 if eng_name in ("gp", "sy") else 1)
        return section

    block.tensor(realize("pe"))
    block.scalar(realize("act"))
    block.vector(realize("dve"))
    block.gpsimd(realize("gp"))
    block.sync(realize("sy"))
    stack.close()
    return nc


_NC_CACHE = {}


def _get_nc(debug=False):
    key = bool(debug)
    if key not in _NC_CACHE:
        _NC_CACHE[key] = build_program(debug=key)
    return _NC_CACHE[key]


def _host_layouts(w_q, w_k, w_v, w_o, rail_maker, rail_up_proj, w_cproj):
    wq_h = np.ascontiguousarray(np.asarray(w_q, dtype=np.float32).reshape(E, E))
    wk_h = np.ascontiguousarray(np.asarray(w_k, dtype=np.float32).reshape(E, E))
    wv_h = np.ascontiguousarray(np.asarray(w_v, dtype=np.float32).reshape(E, E))
    woT_h = np.ascontiguousarray(np.asarray(w_o, dtype=np.float32).reshape(E, E).T)
    wcpT = np.asarray(w_cproj, dtype=np.float32).T
    wcp_h = np.ascontiguousarray(
        wcpT.reshape(128, 128, 8, 128).transpose(2, 1, 0, 3))
    mk = np.asarray(rail_maker, dtype=np.float32)
    up = np.asarray(rail_up_proj, dtype=np.float32)
    mkT_h = np.zeros((128, 1024), dtype=np.float32)
    upT_h = np.zeros((128, 4096), dtype=np.float32)
    for G in range(NG):
        h4, g8 = divmod(G, 8)
        for t in range(4):
            rp = 4 * g8 + t
            r0, r1 = 2 * rp, 2 * rp + 1
            mkT_h[32 * t:32 * t + 16, 32 * G:32 * G + 16] = mk[h4, r0].T
            mkT_h[32 * t + 16:32 * t + 32, 32 * G + 16:32 * G + 32] = mk[h4, r1].T
            upT_h[32 * t:32 * t + 16, 128 * G:128 * G + 64] = up[h4, r0].T
            upT_h[32 * t + 16:32 * t + 32, 128 * G + 64:128 * G + 128] = up[h4, r1].T
    return wq_h, wk_h, wv_h, woT_h, wcp_h, mkT_h, upT_h


def kernel(L0_attn_x_BSE, L0_attn_w_q, L0_attn_w_k, L0_attn_w_v, L0_attn_w_o,
           L0_ffn_rail_maker, L0_ffn_rail_up_proj, L0_ffn_w_cproj):
    x = np.asarray(L0_attn_x_BSE, dtype=np.float32)
    xt_full = np.ascontiguousarray(x.transpose(0, 2, 1))
    wq_h, wk_h, wv_h, woT_h, wcp_h, mkT_h, upT_h = _host_layouts(
        L0_attn_w_q, L0_attn_w_k, L0_attn_w_v, L0_attn_w_o,
        L0_ffn_rail_maker, L0_ffn_rail_up_proj, L0_ffn_w_cproj)
    nc = _get_nc(debug=False)
    in_maps = []
    for c in range(8):
        in_maps.append({
            "xt": np.ascontiguousarray(xt_full[2 * c:2 * c + 2]),
            "wq": wq_h, "wk": wk_h, "wv": wv_h, "woT": woT_h,
            "wcp": wcp_h, "mkT": mkT_h, "upT": upT_h,
        })
    res = run_bass_kernel_spmd(nc, in_maps, core_ids=list(range(8)))
    shards = [np.asarray(r["y"]) for r in res.results]
    out = np.concatenate(shards, axis=0)
    return np.ascontiguousarray(out.transpose(0, 2, 1)).astype(np.float32)
